# version 36
# baseline (speedup 1.0000x reference)
import numpy as np
import ml_dtypes
from contextlib import ExitStack

import concourse.bass as bass
import concourse.mybir as mybir
from concourse.bass_utils import run_bass_kernel_spmd

F32 = mybir.dt.float32
BF16 = mybir.dt.bfloat16
AF = mybir.ActivationFunctionType
ALU = mybir.AluOpType
AX = mybir.AxisListType
NPBF16 = ml_dtypes.bfloat16

D = 4096
NCH = 32
B_ = 4
S = 2048
NTOK = 1024
NSH = 1536
NT_SH = 12
NT_OWN = 8
QOFF = 256
H = 32
EPS = 1e-5
SCALE = float(1.0 / np.sqrt(128.0))

SAME_ENGINE_SYNC = True

ENG_COMPUTE = ("pe", "act", "dve", "pool", "sp")


class Em:

    def __init__(self):
        self.ops = []
        self.res = {}

    def op(self, engine, fn, reads=(), writes=(), dma=None, tag=""):
        opid = len(self.ops)
        deps = set()
        for r in reads:
            st = self.res.get(r)
            if st is not None and st["w"] is not None:
                deps.add(st["w"])
        for w in writes:
            st = self.res.get(w)
            if st is not None:
                if st["w"] is not None:
                    deps.add(st["w"])
                deps.update(st["r"])
        for r in reads:
            self.res.setdefault(r, {"w": None, "r": []})["r"].append(opid)
        for w in writes:
            self.res[w] = {"w": opid, "r": []}
        deps.discard(opid)
        self.ops.append(dict(engine=engine, fn=fn, deps=deps, dma=dma, tag=tag))
        return opid

    def build(self, sems, dsems):
        needed = set()
        for op in self.ops:
            needed |= op["deps"]
        cnt = {e: 0 for e in ENG_COMPUTE}
        dcnt = {q: 0 for q in dsems}
        meta = []
        for i, op in enumerate(self.ops):
            if op["dma"] is not None:
                dcnt[op["dma"]] += 1
                meta.append(("dma", op["dma"], dcnt[op["dma"]]))
            elif i in needed:
                cnt[op["engine"]] += 1
                meta.append(("mark", op["engine"], cnt[op["engine"]]))
            else:
                meta.append(("plain", op["engine"], cnt[op["engine"]]))
        streams = {e: [] for e in ENG_COMPUTE}
        watermark = {e: {} for e in ENG_COMPUTE}
        for i, op in enumerate(self.ops):
            eng = op["engine"]
            waits = {}
            for d in op["deps"]:
                kind, key, val = meta[d]
                if kind == "dma":
                    sk = ("dma", key)
                    waits[sk] = max(waits.get(sk, 0), 16 * val)
                else:
                    deng = self.ops[d]["engine"]
                    if deng == eng and (not SAME_ENGINE_SYNC or deng == "pe"):
                        continue
                    if kind == "plain":
                        raise RuntimeError(f"dep on unmarked op {d} ({self.ops[d]['tag']}) from {i} ({op['tag']})")
                    sk = ("eng", key)
                    waits[sk] = max(waits.get(sk, 0), val)
            wm = watermark[eng]
            emit_waits = []
            for sk, val in sorted(waits.items()):
                if wm.get(sk, 0) < val:
                    wm[sk] = val
                    emit_waits.append((sk, val))
            kind, key, val = meta[i]

            def make(emit_waits=emit_waits, op=op, kind=kind, key=key):
                def run(eh):
                    for (skind, sname), v in emit_waits:
                        sem = dsems[sname] if skind == "dma" else sems[sname]
                        eh.wait_ge(sem, v)
                    inst = op["fn"](eh)
                    if kind == "dma":
                        inst.then_inc(dsems[key], 16)
                    elif kind == "mark":
                        inst.then_inc(sems[key], 1)
                return run
            streams[eng].append(make())
        return streams


def build_program():
    nc = bass.Bass("TRN2", target_bir_lowering=False)

    d_x = nc.dram_tensor("x_sh", [NSH, D], F32, kind="ExternalInput")
    d_wq = nc.dram_tensor("wq_l", [NCH, 128, NCH * 128], F32, kind="ExternalInput")
    d_wo = nc.dram_tensor("wo_l", [NCH, 128, NCH * 128], F32, kind="ExternalInput")
    d_wb = nc.dram_tensor("wb_l", [NCH, 128, NCH * 128], F32, kind="ExternalInput")
    d_wk = nc.dram_tensor("wk_l", [128, NCH * 128], F32, kind="ExternalInput")
    d_wv = nc.dram_tensor("wv_l", [128, NCH * 128], F32, kind="ExternalInput")
    d_wr = nc.dram_tensor("wr_l", [128, NCH * 8], F32, kind="ExternalInput")
    d_aa = nc.dram_tensor("aa_l", [128, NCH * 128], F32, kind="ExternalInput")
    d_ba = nc.dram_tensor("ba_l", [128, NCH * 128], F32, kind="ExternalInput")
    d_bb = nc.dram_tensor("bb_l", [128, NCH], F32, kind="ExternalInput")
    d_cos = nc.dram_tensor("cos_l", [128, NSH], F32, kind="ExternalInput")
    d_sin = nc.dram_tensor("sin_l", [128, NSH], F32, kind="ExternalInput")
    d_wvv = nc.dram_tensor("wv_c", [128, 1536], F32, kind="ExternalInput")
    d_pad = nc.dram_tensor("pad_c", [1, 512], BF16, kind="ExternalInput")
    d_idb = nc.dram_tensor("idb_c", [128, 128], BF16, kind="ExternalInput")
    d_idf = nc.dram_tensor("idf_c", [128, 128], F32, kind="ExternalInput")
    d_ones = nc.dram_tensor("ones_c", [128, 128], BF16, kind="ExternalInput")
    d_o1 = nc.dram_tensor("o1_c", [1, 128], BF16, kind="ExternalInput")
    d_e8 = nc.dram_tensor("e8_c", [8, 128], F32, kind="ExternalInput")

    d_out = nc.dram_tensor("out_sh", [NTOK, D], F32, kind="ExternalOutput")

    d_raw1 = nc.dram_tensor("raw1_d", [128, NT_OWN, NCH, 128], F32)
    d_raw2 = nc.dram_tensor("raw2_d", [128, NT_OWN, NCH, 128], F32)
    d_x2 = nc.dram_tensor("x2_d", [NT_OWN, 128, D], F32)

    stack = ExitStack()
    sb = lambda nm, sh, dt=F32: stack.enter_context(nc.sbuf_tensor(nm, sh, dt))

    cosT = sb("cosT", [128, NSH])
    sinT = sb("sinT", [128, NSH])
    idb = sb("idb", [128, 128], BF16)
    idf = sb("idf", [128, 128])
    onesb = sb("onesb", [128, 128], BF16)
    o1b = sb("o1b", [1, 128], BF16)
    e8f = sb("e8f", [8, 128])
    wvc = sb("wvc", [128, 1536])
    padr = sb("padr", [1, 512], BF16)
    wrb = sb("wrb", [128, NCH * 8], BF16)
    biasl = sb("biasl", [128, NCH])
    wbuf = [sb("wbuf0", [128, NCH * 128], BF16), sb("wbuf1", [128, NCH * 128], BF16)]
    kwb = sb("kwb", [128, NCH * 128], BF16)
    BIGA = sb("BIGA", [128, 32 * 1024], BF16)
    BIGB = sb("BIGB", [128, 32 * 1024], BF16)
    SCR = sb("SCR", [128, 8 * 1024], BF16)
    kr = sb("kr", [128, NSH], BF16)
    vt = sb("vt", [128, NSH], BF16)
    bn6_ = [sb("bn6a", [128, 48]), sb("bn6b", [128, 48])]
    mv_ = [sb("mva", [128, 2]), sb("mvb", [128, 2])]
    sqv_ = [sb("sqva", [128, 1]), sb("sqvb", [128, 1])]
    rstd_ = [sb("rstda", [128, 1]), sb("rstdb", [128, 1])]
    nbias_ = [sb("nbiasa", [128, 1]), sb("nbiasb", [128, 1])]
    csb = [sb("csb0", [128, 512]), sb("csb1", [128, 512]), sb("csb2", [128, 512])]
    tails = [sb(f"tail{k}", [128, 256]) for k in range(3)]
    lgT = SCR[0:8, 0:2048].bitcast(F32)
    cmbT = lgT
    lgtm = sb("lgtm", [128, 64])
    cmbtm = sb("cmbtm", [128, 64])
    mx1 = sb("mx1", [128, 1])
    mx8 = sb("mx8", [128, 8])
    scr8 = sb("scr8", [128, 8])
    mr8 = sb("mr8", [128, 8])
    ee8 = sb("ee8", [128, 8])
    den = sb("den", [128, 1])
    rden1 = sb("rden1", [128, 1])
    epsv = sb("epsv", [128, 1])

    psA = stack.enter_context(nc.psum_tensor("psA", [128, 1024], F32))
    psB = stack.enter_context(nc.psum_tensor("psB", [128, 1024], F32))
    psC = stack.enter_context(nc.psum_tensor("psC", [128, 512], F32))
    psD = stack.enter_context(nc.psum_tensor("psD", [128, 512], F32))
    psE = stack.enter_context(nc.psum_tensor("psE", [128, 512], F32))
    psF = stack.enter_context(nc.psum_tensor("psF", [128, 512], F32))

    sems = {e: stack.enter_context(nc.semaphore(f"s_{e}")) for e in ENG_COMPUTE}
    dsems = {"sp": stack.enter_context(nc.semaphore("d_sp")),
             "pool": stack.enter_context(nc.semaphore("d_pool")),
             "actq": stack.enter_context(nc.semaphore("d_actq")),
             "st": stack.enter_context(nc.semaphore("d_st"))}

    em = Em()

    def _regs(pref, lo, hi, g):
        return [f"{pref}{k}" for k in range(lo // g, (hi + g - 1) // g)]

    def RA(lo, hi):
        return _regs("A", lo, hi, 512)

    def RB(lo, hi):
        return _regs("B", lo, hi, 512)

    def RS(lo, hi):
        return _regs("S", lo, hi, 512)

    def bigb_f32(col_f32, n_f32):
        return BIGB[:, 2 * col_f32: 2 * (col_f32 + n_f32)].bitcast(F32)

    def scr_f32(col_f32, n_f32):
        return SCR[:, 2 * col_f32: 2 * (col_f32 + n_f32)].bitcast(F32)

    PS = {"pA0": psA[:, 0:512], "pA1": psA[:, 512:1024],
          "pB0": psB[:, 0:512], "pB1": psB[:, 512:1024],
          "pC": psC[:, :], "pD": psD[:, :], "pE": psE[:, :], "pF": psF[:, :]}

    def dma(queue, dst, src, reads=(), writes=(), tag=""):
        host = {"sp": "sp", "pool": "pool", "actq": "act", "st": "sp"}[queue]
        return em.op(host, lambda e, d=dst, s=src: e.dma_start(d, s),
                     reads=reads, writes=writes, dma=queue, tag=tag)

    def w_prefetch(d_w, j, tagp):
        wres = [f"wb{j % 2}"]
        dma("pool", wbuf[j % 2][:, :], d_w[j, :, :], reads=wres, writes=wres, tag=f"{tagp}.w{j}")

    dma("sp", cosT[:, :], d_cos[:, :], writes=["cosT"])
    dma("sp", sinT[:, :], d_sin[:, :], writes=["sinT"])
    dma("sp", idb[:, :], d_idb[:, :], writes=["idb"])
    dma("sp", idf[:, :], d_idf[:, :], writes=["idf"])
    dma("sp", onesb[:, :], d_ones[:, :], writes=["onesb"])
    dma("sp", o1b[:, :], d_o1[:, :], writes=["o1b"])
    dma("sp", e8f[:, :], d_e8[:, :], writes=["e8f"])
    dma("sp", wvc[:, :], d_wvv[:, :], writes=["wvc"])
    dma("sp", padr[:, :], d_pad[:, :], writes=["padr"])
    dma("sp", biasl[:, :], d_bb[:, :], writes=["biasl"])
    em.op("dve", lambda e: e.memset(epsv[:, :], float(EPS)), reads=[], writes=["epsv"], tag="init.eps")
    dma("pool", wrb[:, :], d_wr[:, :], writes=["wrb"])
    dma("pool", kwb[:, :], d_wk[:, :], writes=["kwb"])

    def emit_bn_chunk(src_ap, src_res, par, cc, tagp):
        bn6 = bn6_[par]
        sl = src_ap if src_ap.shape[-1] == 512 else src_ap[:, cc * 512:(cc + 1) * 512]
        em.op("dve", lambda e, cc=cc, sl=sl, bn6=bn6: e.bn_stats(
            bn6[:, cc * 6:(cc + 1) * 6], sl),
            reads=src_res, writes=[f"bn6{par}.{cc}"], tag=f"{tagp}.bn{cc}")

    def emit_ln_finish(par, tagp):
        bn6, mv, sqv, rstd, nbias = bn6_[par], mv_[par], sqv_[par], rstd_[par], nbias_[par]
        em.op("dve", lambda e, bn6=bn6, mv=mv: e.bn_aggr(mv[:, :], bn6.rearrange("p (c s) -> p c s", s=6)),
              reads=[f"bn6{par}.{cc}" for cc in range(8)], writes=[f"mv{par}"], tag=f"{tagp}.bnagg")
        em.op("act", lambda e, mv=mv, sqv=sqv: e.activation(sqv[:, :], mv[:, 1:2], AF.Sqrt, bias=epsv[:, 0:1], scale=1.0),
              reads=[f"mv{par}", "epsv"], writes=[f"sqv{par}"], tag=f"{tagp}.sqrt")
        em.op("dve", lambda e, sqv=sqv, rstd=rstd: e.reciprocal(rstd[:, :], sqv[:, :]),
              reads=[f"sqv{par}"], writes=[f"rstd{par}"], tag=f"{tagp}.rcp")
        em.op("dve", lambda e, mv=mv, rstd=rstd, nbias=nbias: e.tensor_scalar(
            nbias[:, :], mv[:, 0:1], rstd[:, 0:1], -1.0, ALU.mult, ALU.mult),
            reads=[f"mv{par}", f"rstd{par}"], writes=[f"nbias{par}"], tag=f"{tagp}.nb")

    def emit_ln_stats(src_ap, src_res, tagp, par=0):
        for cc in range(8):
            emit_bn_chunk(src_ap, src_res, par, cc, tagp)
        emit_ln_finish(par, tagp)

    tr_banks = ["pC", "pD", "pE", "pF"]

    def xslot(i):
        return bigb_f32(4096 * (i % 2), 4096)

    def xslot_cols(i):
        return (8192 * (i % 2), 8192 * (i % 2) + 8192)

    hxF = BIGB[:, 16384:24576]
    hxB = BIGB[:, 24576:32768]

    OWN = list(range(2, 10))

    def ln1t_buf(i):
        return SCR[:, 4096 * (i % 2): 4096 * (i % 2) + 4096]

    def ln1t_cols(i):
        return (4096 * (i % 2), 4096 * (i % 2) + 4096)

    HALO_X = {0: (8192, (16384, 24576)), 10: (12288, (24576, 32768))}

    def xbuf_of(t):
        if t in HALO_X:
            off, (lo, hi) = HALO_X[t]
            return bigb_f32(off, 4096), RB(lo, hi)
        if t in (1, 11):
            return xslot(1), RB(*xslot_cols(1))
        return xslot(t), RB(*xslot_cols(t))

    def s1_dma(t):
        xap, xr = xbuf_of(t)
        dma("sp", xap, d_x[t * 128:(t + 1) * 128, :], reads=xr, writes=xr, tag=f"A.x{t}")

    def s1_stats(t):
        xap, xr = xbuf_of(t)
        emit_ln_stats(xap, xr, f"A.{t}", par=t % 2)
        lr = RS(*ln1t_cols(t))
        em.op("act", lambda e, t=t, xap=xap: e.activation(ln1t_buf(t), xap, AF.Identity,
                                                 bias=nbias_[t % 2][:, 0:1], scale=rstd_[t % 2][:, 0:1]),
              reads=xr + [f"rstd{t % 2}", f"nbias{t % 2}"], writes=lr, tag=f"A.ap{t}")

    def s1_tr(t):
        lr = RS(*ln1t_cols(t))
        for cg in range(0, 32, 8):
            bank = tr_banks[(cg // 8) % 4]
            bview = PS[bank].bitcast(BF16)
            for c in range(cg, cg + 8):
                em.op("pe", lambda e, c=c, cg=cg, t=t, bv=bview: e.transpose(
                    bv[:, (c - cg) * 128:(c - cg + 1) * 128],
                    ln1t_buf(t)[:, c * 128:(c + 1) * 128], idb[:, :]),
                    reads=lr + ["idb"], writes=[bank], tag=f"A.tr{t}.{c}")
            if t in OWN:
                to = t - 2
                out_ap = BIGA.rearrange("p (c n) -> p c n", n=1024)[:, cg:cg + 8, to * 128:(to + 1) * 128]
                wres = RA(cg * 1024, (cg + 8) * 1024)
            else:
                hx = hxF if t < 2 else hxB
                base = 16384 if t < 2 else 24576
                ti = t % 2
                out_ap = hx.rearrange("p (c n) -> p c n", n=256)[:, cg:cg + 8, ti * 128:(ti + 1) * 128]
                wres = RB(base + cg * 256, base + (cg + 8) * 256)
            em.op("dve", lambda e, bv=bview, oa=out_ap: e.tensor_copy(
                oa, bv.rearrange("p (c n) -> p c n", n=128)),
                reads=[bank], writes=wres, tag=f"A.trev{t}.{cg}")

    s1_dma(OWN[0])
    tpend = None
    for j, t in enumerate(OWN):
        if j + 1 < len(OWN):
            s1_dma(OWN[j + 1])
        s1_stats(t)
        if tpend is not None:
            s1_tr(tpend)
        tpend = t
    s1_tr(tpend)

    KVG = [
        (lambda c: hxF[:, c * 256:(c + 1) * 256], lambda c: RB(16384 + c * 256, 16384 + (c + 1) * 256), 0, 256, [0, 1]),
        (lambda c: BIGA[:, c * 1024 + 0: c * 1024 + 512], lambda c: RA(c * 1024, c * 1024 + 512), 256, 512, [2, 3, 4, 5]),
        (lambda c: BIGA[:, c * 1024 + 512: c * 1024 + 1024], lambda c: RA(c * 1024 + 512, c * 1024 + 1024), 768, 512, [6, 7, 8, 9]),
        (lambda c: hxB[:, c * 256:(c + 1) * 256], lambda c: RB(24576 + c * 256, 24576 + (c + 1) * 256), 1280, 256, [10, 11]),
    ]
    vsb5 = csb[2][:, :].bitcast(BF16)[:, 0:512]

    def kv_k(gi):
        (rhs_fn, rres_fn, klo, n, ktiles) = KVG[gi]
        krres = [f"kr{t}" for t in ktiles]
        for c in range(NCH):
            em.op("pe", lambda e, c=c, rhs_fn=rhs_fn, n=n: e.matmul(
                psC[:, 0:n], kwb[:, c * 128:(c + 1) * 128], rhs_fn(c),
                start=(c == 0), stop=(c == 31)),
                reads=["kwb"] + rres_fn(c), writes=["pC"], tag=f"A.k{gi}.{c}")
        qc = csb[0][:, 0:n]
        m2 = csb[1][:, 0:n]
        cs = cosT[:, klo:klo + n]
        sn = sinT[:, klo:klo + n]
        em.op("dve", lambda e, cs=cs, qc=qc, n=n: e.tensor_tensor(qc, psC[:, 0:n], cs, ALU.mult),
              reads=["pC", "cosT"], writes=["csb0"], tag=f"A.rk1.{gi}")
        em.op("dve", lambda e, sn=sn, m2=m2, n=n: e.tensor_tensor(m2[0:64, :], psC[64:128, 0:n], sn[0:64, :], ALU.mult),
              reads=["pC", "sinT"], writes=["csb1"], tag=f"A.rk2.{gi}")
        em.op("dve", lambda e, sn=sn, m2=m2, n=n: e.tensor_tensor(m2[64:128, :], psC[0:64, 0:n], sn[64:128, :], ALU.mult),
              reads=["pC", "sinT"], writes=["csb1b"], tag=f"A.rk3.{gi}")
        em.op("dve", lambda e, klo=klo, n=n, qc=qc, m2=m2: e.tensor_tensor(kr[:, klo:klo + n], qc, m2, ALU.add),
              reads=["csb0", "csb1", "csb1b"], writes=krres, tag=f"A.rk4.{gi}")

    def kv_v(gi):
        (rhs_fn, rres_fn, klo, n, ktiles) = KVG[gi]
        vtres = [f"vt{t}" for t in ktiles]
        for c in range(NCH):
            em.op("pe", lambda e, c=c, rhs_fn=rhs_fn, n=n: e.matmul(
                psD[:, 0:n], kwb[:, c * 128:(c + 1) * 128], rhs_fn(c),
                start=(c == 0), stop=(c == 31)),
                reads=["kwb"] + rres_fn(c), writes=["pD"], tag=f"A.v{gi}.{c}")
        em.op("act", lambda e, n=n: e.copy(vsb5[:, 0:n], psD[:, 0:n]),
              reads=["pD"], writes=["csb2"], tag=f"A.vc{gi}")
        for k in range(n // 128):
            bank = ["pE", "pF"][k % 2]
            bview = PS[bank].bitcast(BF16)
            em.op("pe", lambda e, k=k, bv=bview: e.transpose(
                bv[:, 0:128], vsb5[:, k * 128:(k + 1) * 128], idb[:, :]),
                reads=["csb2", "idb"], writes=[bank], tag=f"A.vt{gi}.{k}")
            em.op("dve", lambda e, k=k, klo=klo, bv=bview: e.tensor_copy(
                vt[:, klo + k * 128: klo + (k + 1) * 128], bv[:, 0:128]),
                reads=[bank], writes=vtres, tag=f"A.vte{gi}.{k}")

    binserts = {
        0: lambda: (s1_dma(0), s1_dma(10), s1_stats(0)),
        1: lambda: s1_tr(0),
        2: lambda: (s1_dma(1), s1_stats(10)),
        3: lambda: s1_tr(10),
        4: lambda: s1_stats(1),
        5: lambda: (s1_tr(1), s1_dma(11)),
        6: lambda: s1_stats(11),
        7: lambda: s1_tr(11),
        8: lambda: (kv_k(1), kv_k(2)),
        9: lambda: (kv_k(0), kv_k(3),
                    dma("pool", kwb[:, :], d_wv[:, :], reads=["kwb"], writes=["kwb"], tag="A.wv")),
        10: lambda: (kv_v(1), kv_v(2)),
        11: lambda: (kv_v(0), kv_v(3)),
    }

    def qr_head(h):
        return BIGB[:, h * 1024:(h + 1) * 1024]

    for h in range(H):
        if h in binserts:
            binserts[h]()
        wres = [f"wb{h % 2}"]
        dma("pool", wbuf[h % 2][:, :], d_wq[h, :, :], reads=wres, writes=wres, tag=f"B.w{h}")
        pX = psA if h % 2 == 0 else psB
        pXr = ["pA0", "pA1"] if h % 2 == 0 else ["pB0", "pB1"]
        for c in range(NCH):
            for nh in range(2):
                em.op("pe", lambda e, c=c, nh=nh, h=h, pX=pX: e.matmul(
                    pX[:, nh * 512:(nh + 1) * 512], wbuf[h % 2][:, c * 128:(c + 1) * 128],
                    BIGA[:, c * 1024 + nh * 512: c * 1024 + (nh + 1) * 512],
                    start=(c == 0), stop=(c == 31)),
                    reads=wres + RA(c * 1024 + nh * 512, c * 1024 + (nh + 1) * 512),
                    writes=[pXr[nh]], tag=f"B.q{h}.{c}.{nh}")
        for nh in range(2):
            qc = csb[0][:, :]
            m2 = csb[1][:, :]
            cs = cosT[:, QOFF + nh * 512: QOFF + (nh + 1) * 512]
            sn = sinT[:, QOFF + nh * 512: QOFF + (nh + 1) * 512]
            pslice = pX[:, nh * 512:(nh + 1) * 512]
            qres = RB(h * 1024 + nh * 512, h * 1024 + (nh + 1) * 512)
            em.op("dve", lambda e, pslice=pslice, cs=cs, qc=qc: e.tensor_tensor(qc, pslice, cs, ALU.mult),
                  reads=[pXr[nh], "cosT"], writes=["csb0"], tag=f"B.r1.{h}.{nh}")
            em.op("dve", lambda e, pslice=pslice, sn=sn, m2=m2: e.tensor_tensor(
                m2[0:64, :], pslice[64:128, :], sn[0:64, :], ALU.mult),
                reads=[pXr[nh], "sinT"], writes=["csb1"], tag=f"B.r2.{h}.{nh}")
            em.op("dve", lambda e, pslice=pslice, sn=sn, m2=m2: e.tensor_tensor(
                m2[64:128, :], pslice[0:64, :], sn[64:128, :], ALU.mult),
                reads=[pXr[nh], "sinT"], writes=["csb1b"], tag=f"B.r3.{h}.{nh}")
            em.op("dve", lambda e, h=h, nh=nh, qc=qc, m2=m2: e.tensor_tensor(
                qr_head(h)[:, nh * 512:(nh + 1) * 512], qc, m2, ALU.add),
                reads=["csb0", "csb1", "csb1b"] + qres, writes=qres, tag=f"B.r4.{h}.{nh}")

    WIN = []
    for i in range(5):
        qlo = max(0, 256 * i - 256)
        qhi = min(NTOK, 256 * i + 256)
        Areg = (256 * i - 256, 256 * i) if i >= 1 else None
        Breg = (256 * i, 256 * i + 256) if i <= 3 else None
        WIN.append((qlo, qhi - qlo, Areg, Breg))

    def attnT_buf(par, kt):
        base = 4096 + par * 2048
        return SCR[:, base + kt * 512: base + (kt + 1) * 512]

    def attnT_res(par, kt):
        return [f"S{8 + par * 4 + kt}"]

    sc_banks = ["pC", "pD", "pA0", "pA1"]
    sc_ps = [psC[:, :], psD[:, :], psA[:, 0:512], psA[:, 512:1024]]

    sm_banks = [("pE", psE[:, 0:512]), ("pB0", psB[:, 0:512]), ("pA0", psA[:, 0:512])]
    av_banks = [("pF", psF[:, 0:512]), ("pB1", psB[:, 512:1024]), ("pA1", psA[:, 512:1024])]
    sc_bk = [("pC", psC[:, :]), ("pD", psD[:, :])]

    def avsb_buf(k):
        return scr_f32(512 * k, 512)

    def avsb_res(k):
        return [f"S{2 * k}", f"S{2 * k + 1}"]

    def post_pe(p):
        (h, i, par3, par2, qlo, qcnt, Areg, Breg) = p
        smbk, smps = sm_banks[par3]
        avbk, avps = av_banks[par3]
        for kt in range(4):
            ktile = (2 * i + kt) * 128
            em.op("pe", lambda e, par2=par2, kt=kt, qcnt=qcnt, smps=smps, i=i: e.matmul(
                smps[:, 0:qcnt], onesb[:, :], attnT_buf(par2, kt)[:, 0:qcnt],
                start=(kt == 0), stop=(kt == 3 and i != 4)),
                reads=attnT_res(par2, kt) + ["onesb"], writes=[smbk], tag=f"C.sm{h}.{i}.{kt}")
            em.op("pe", lambda e, par2=par2, kt=kt, ktile=ktile, qcnt=qcnt, avps=avps: e.matmul(
                avps[:, 0:qcnt], vt[:, ktile:ktile + 128], attnT_buf(par2, kt)[:, 0:qcnt],
                start=(kt == 0), stop=(kt == 3)),
                reads=attnT_res(par2, kt) + [f"vt{ktile // 128}"], writes=[avbk], tag=f"C.av{h}.{i}.{kt}")
        if i == 4:
            em.op("pe", lambda e, qcnt=qcnt, smps=smps: e.matmul(
                smps[:, 0:qcnt], o1b[:, :], padr[0:1, 0:qcnt],
                start=False, stop=True),
                reads=["padr", "o1b"], writes=[smbk], tag=f"C.pd{h}.{i}")

    def post_tail(p):
        (h, i, par3, par2, qlo, qcnt, Areg, Breg) = p
        smbk, smps = sm_banks[par3]
        avbk, avps = av_banks[par3]
        cb = csb[par3]
        cbr = f"csb{par3}"
        wcl = [cbr, "csb1b"] if par3 == 1 else [cbr]
        wvo = [0, 256, 768, 768, 1280][i]
        em.op("dve", lambda e, cb=cb, qcnt=qcnt, smps=smps: e.reciprocal(cb[:, 0:qcnt], smps[:, 0:qcnt]),
              reads=[smbk], writes=wcl, tag=f"C.rc{h}.{i}")
        cweng = "dve" if (5 * h + i) % 4 == 3 else "pool"
        em.op(cweng, lambda e, cb=cb, qcnt=qcnt, wvo=wvo: e.tensor_tensor(
            cb[:, 0:qcnt], cb[:, 0:qcnt], wvc[:, wvo:wvo + qcnt], ALU.mult),
            reads=[cbr, "wvc"], writes=[cbr], tag=f"C.cw{h}.{i}")
        avs = avsb_buf(par3)
        em.op("act", lambda e, avs=avs, qcnt=qcnt, avps=avps: e.copy(avs[:, 0:qcnt], avps[:, 0:qcnt]),
              reads=[avbk], writes=avsb_res(par3), tag=f"C.ave{h}.{i}")
        tl = tails[h % 3]
        tlr = f"tail{h % 3}"
        if Areg is not None:
            aoff = Areg[0] - qlo
            ares = RA(h * 1024 + Areg[0], h * 1024 + Areg[1])
            em.op("pool", lambda e, cb=cb, aoff=aoff, avs=avs: e.tensor_tensor(
                cb[:, aoff:aoff + 256], cb[:, aoff:aoff + 256], avs[:, aoff:aoff + 256], ALU.mult),
                reads=[cbr] + avsb_res(par3), writes=[cbr], tag=f"C.t1{h}.{i}")
            em.op("pool", lambda e, h=h, Areg=Areg, cb=cb, aoff=aoff, tl=tl: e.tensor_tensor(
                BIGA[:, h * 1024 + Areg[0]: h * 1024 + Areg[1]], cb[:, aoff:aoff + 256], tl[:, :], ALU.add),
                reads=[cbr, tlr] + ares, writes=ares, tag=f"C.bA{h}.{i}")
        if Breg is not None:
            boff = Breg[0] - qlo
            em.op("pool", lambda e, cb=cb, boff=boff, avs=avs, tl=tl: e.tensor_tensor(
                tl[:, :], avs[:, boff:boff + 256], cb[:, boff:boff + 256], ALU.mult),
                reads=[cbr] + avsb_res(par3), writes=[tlr], tag=f"C.bB{h}.{i}")

    w_prefetch(d_wo, 0, "D")
    w_prefetch(d_wo, 1, "D")
    pend = None
    tailq = []
    idx = 0
    for h in range(H):
        for i, (qlo, qcnt, Areg, Breg) in enumerate(WIN):
            par3 = idx % 3
            par2 = idx % 2
            for kt in range(4):
                bk, bp = sc_bk[kt % 2]
                ktile = (2 * i + kt) * 128
                em.op("pe", lambda e, bp=bp, ktile=ktile, h=h, qlo=qlo, qcnt=qcnt: e.matmul(
                    bp[:, 0:qcnt], kr[:, ktile:ktile + 128],
                    qr_head(h)[:, qlo:qlo + qcnt], start=True, stop=True),
                    reads=[f"kr{ktile // 128}"] + RB(h * 1024 + qlo, h * 1024 + qlo + qcnt),
                    writes=[bk], tag=f"C.sc{h}.{i}.{kt}")
                em.op("act", lambda e, bp=bp, par2=par2, kt=kt, qcnt=qcnt: e.activation(
                    attnT_buf(par2, kt)[:, 0:qcnt], bp[:, 0:qcnt], AF.Sigmoid, scale=SCALE),
                    reads=[bk], writes=attnT_res(par2, kt), tag=f"C.sg{h}.{i}.{kt}")
            if pend is not None:
                post_pe(pend)
                tailq.append(pend)
                if len(tailq) == 2:
                    post_tail(tailq.pop(0))
            pend = (h, i, par3, par2, qlo, qcnt, Areg, Breg)
            idx += 1
    post_pe(pend)
    tailq.append(pend)
    for t in tailq:
        post_tail(t)

    def slab_buf(par):
        return scr_f32(par * 1024, 1024)

    def slab_res(par):
        return [f"S{par}"]

    def big_matmul_spill(d_w, d_raw, tagp, lora=False, npre=0):
        spills = []
        for j in range(NCH):
            wres = [f"wb{j % 2}"]
            if j >= npre:
                dma("pool", wbuf[j % 2][:, :], d_w[j, :, :], reads=wres, writes=wres, tag=f"{tagp}.w{j}")
            pX = psA if j % 2 == 0 else psB
            pXr = ["pA0", "pA1"] if j % 2 == 0 else ["pB0", "pB1"]
            for c in range(NCH):
                for nh in range(2):
                    em.op("pe", lambda e, c=c, nh=nh, j=j, pX=pX: e.matmul(
                        pX[:, nh * 512:(nh + 1) * 512], wbuf[j % 2][:, c * 128:(c + 1) * 128],
                        BIGA[:, c * 1024 + nh * 512: c * 1024 + (nh + 1) * 512],
                        start=(c == 0), stop=(c == 31 and not lora)),
                        reads=wres + RA(c * 1024 + nh * 512, c * 1024 + (nh + 1) * 512),
                        writes=[pXr[nh]], tag=f"{tagp}.m{j}.{c}.{nh}")
            if lora:
                for nh in range(2):
                    em.op("pe", lambda e, nh=nh, j=j, pX=pX: e.matmul(
                        pX[:, nh * 512:(nh + 1) * 512], BIGB[:, 4096 + j * 128: 4096 + (j + 1) * 128],
                        SCR[:, 6144 + nh * 512: 6144 + (nh + 1) * 512], start=False, stop=True),
                        reads=RB(4096 + j * 128, 4096 + (j + 1) * 128) + ["babld", "S12", "S13"],
                        writes=[pXr[nh]], tag=f"{tagp}.l{j}.{nh}")
            sl = slab_buf(j % 2)
            slr = slab_res(j % 2)
            if lora:
                em.op("act", lambda e, sl=sl, pX=pX, j=j: e.activation(
                    sl, pX[:, :], AF.Identity, bias=biasl[:, j:j + 1], scale=1.0),
                    reads=pXr + ["biasl"], writes=slr, tag=f"{tagp}.e{j}")
            else:
                em.op("act", lambda e, sl=sl, pX=pX: e.copy(sl, pX[:, :]),
                      reads=pXr, writes=slr, tag=f"{tagp}.e{j}")
            spills.append(dma("sp", d_raw[:, :, j, :], sl.rearrange("p (t n) -> p t n", n=128),
                              reads=slr, writes=[], tag=f"{tagp}.s{j}"))
        return spills

    spillD = big_matmul_spill(d_wo, d_raw1, "D", npre=2)

    def apply_phase(d_raw, tagp, with_ln3, spill_ids, x_src_fn, out_sink_fn):
        def tile_bufs(to):
            rawfm = bigb_f32(4096 * (to % 2), 4096)
            rfr = RB(*xslot_cols(to))
            xt = bigb_f32(8192 + 4096 * (to % 2), 4096)
            xtr = RB(16384 + 8192 * (to % 2), 24576 + 8192 * (to % 2))
            return rawfm, rfr, xt, xtr

        def emit_dmas(to):
            rawfm, rfr, xt, xtr = tile_bufs(to)
            rb = dma("actq", rawfm.rearrange("p (j n) -> p j n", n=128),
                     d_raw[:, to, :, :], reads=rfr, writes=rfr, tag=f"{tagp}.rb{to}")
            em.ops[rb]["deps"].update(spill_ids)
            x_src_fn(to, xt, xtr)

        all_banks = ["pC", "pD", "pE", "pF", "pA0", "pA1", "pB0", "pB1"]
        deferred = []
        emit_dmas(0)
        for to in range(NT_OWN):
            if to + 1 < NT_OWN:
                emit_dmas(to + 1)
            rawfm, rfr, xt, xtr = tile_bufs(to)
            par = to % 2
            lo0 = 8192 * (to % 2)
            for jg in range(0, 32, 4):
                bank = all_banks[jg // 4]
                bp = PS[bank]
                for j in range(jg, jg + 4):
                    em.op("pe", lambda e, j=j, jg=jg, bp=bp, rawfm=rawfm: e.transpose(
                        bp[:, (j - jg) * 128:(j - jg + 1) * 128], rawfm[:, j * 128:(j + 1) * 128], idf[:, :]),
                        reads=rfr + ["idf"], writes=[bank], tag=f"{tagp}.tr{to}.{j}")
                emit_bn_chunk(bp, [bank], par, jg // 4, f"{tagp}.{to}")
            emit_ln_finish(par, f"{tagp}.{to}")
            em.op("act", lambda e, xt=xt, par=par: e.activation(
                xt, xt, AF.Identity, bias=nbias_[par][:, 0:1], scale=1.0),
                reads=xtr + [f"nbias{par}"], writes=xtr, tag=f"{tagp}.xb{to}")
            for jg in range(0, 32, 4):
                bank = all_banks[jg // 4]
                bp = PS[bank]
                trr = RB(lo0 + jg * 256, lo0 + (jg + 4) * 256)
                em.op("dve", lambda e, bp=bp, jg=jg, rawfm=rawfm, xt=xt, par=par: e.scalar_tensor_tensor(
                    rawfm[:, jg * 128:(jg + 4) * 128], bp[:, :], rstd_[par][:, 0:1],
                    xt[:, jg * 128:(jg + 4) * 128], ALU.mult, ALU.add),
                    reads=[bank, f"rstd{par}"] + xtr + trr, writes=trr, tag=f"{tagp}.st{to}.{jg}")
            if deferred:
                deferred.pop(0)()
            out_sink_fn(to, rawfm, rfr)
            if with_ln3:
                emit_ln_stats(rawfm, rfr, f"{tagp}.3.{to}", par=par)
                lt = SCR[:, 4096 * (to % 2):4096 * (to % 2) + 4096]
                ltr = RS(*ln1t_cols(to))
                em.op("act", lambda e, rawfm=rawfm, lt=lt, par=par: e.activation(
                    lt, rawfm, AF.Identity, bias=nbias_[par][:, 0:1], scale=rstd_[par][:, 0:1]),
                    reads=rfr + [f"rstd{par}", f"nbias{par}"], writes=ltr, tag=f"{tagp}.l3{to}")
                def l3tr(to=to, lt=lt, ltr=ltr):
                    for cg in range(0, 32, 8):
                        bank = all_banks[cg // 8]
                        bview = PS[bank].bitcast(BF16)
                        for c in range(cg, cg + 8):
                            em.op("pe", lambda e, c=c, cg=cg, bv=bview, lt=lt: e.transpose(
                                bv[:, (c - cg) * 128:(c - cg + 1) * 128], lt[:, c * 128:(c + 1) * 128], idb[:, :]),
                                reads=ltr + ["idb"], writes=[bank], tag=f"{tagp}.l3t{to}.{c}")
                        out_ap = BIGA.rearrange("p (c n) -> p c n", n=1024)[:, cg:cg + 8, to * 128:(to + 1) * 128]
                        leng = "dve" if (cg // 8) % 2 == 0 else "act"
                        lfn = (lambda e, bv=bview, oa=out_ap: e.tensor_copy(oa, bv.rearrange("p (c n) -> p c n", n=128))) \
                            if leng == "dve" else \
                            (lambda e, bv=bview, oa=out_ap: e.copy(oa, bv.rearrange("p (c n) -> p c n", n=128)))
                        em.op(leng, lfn,
                            reads=[bank], writes=RA(cg * 1024, (cg + 8) * 1024), tag=f"{tagp}.l3e{to}.{cg}")
                deferred.append(l3tr)
        while deferred:
            deferred.pop(0)()

    def flush_deferred():
        pass

    def x_from_input(to, xt, xtr):
        dma("actq", xt, d_x[QOFF + to * 128: QOFF + (to + 1) * 128, :],
            reads=xtr, writes=xtr, tag=f"E.x{to}")

    x2spills = {}

    def x2_sink(to, x2ap, x2res):
        x2spills[to] = dma("sp", d_x2[to], x2ap, reads=x2res, writes=[], tag=f"E.x2s{to}")

    apply_phase(d_raw1, "E", True, spillD, x_from_input, x2_sink)

    aab = BIGB[:, 0:4096]
    bab = BIGB[:, 4096:8192]
    hw_ap = SCR[:, 6144:7168]
    c128 = scr_f32(2048, 1024)

    dma("pool", aab, d_aa[:, :], reads=RB(0, 4096), writes=RB(0, 4096) + ["aabld"], tag="F.aa")
    dma("pool", bab, d_ba[:, :], reads=RB(4096, 8192), writes=RB(4096, 8192) + ["babld"], tag="F.ba")
    w_prefetch(d_wb, 0, "G")
    w_prefetch(d_wb, 1, "G")

    for c in range(NCH):
        for nh, (bk, bp) in enumerate([("pE", psE), ("pF", psF)]):
            em.op("pe", lambda e, c=c, nh=nh, bp=bp: e.matmul(
                bp[0:8, 0:512], wrb[:, c * 8:(c + 1) * 8],
                BIGA[:, c * 1024 + nh * 512: c * 1024 + (nh + 1) * 512],
                start=(c == 0), stop=(c == 31)),
                reads=["wrb"] + RA(c * 1024 + nh * 512, c * 1024 + (nh + 1) * 512),
                writes=[bk], tag=f"F.r{c}.{nh}")
    em.op("dve", lambda e: e.tensor_copy(lgT[:, 0:512], psE[0:8, 0:512]),
          reads=["pE"], writes=["lgT.0", "S0", "S1"], tag="F.le0")
    em.op("dve", lambda e: e.tensor_copy(lgT[:, 512:1024], psF[0:8, 0:512]),
          reads=["pF"], writes=["lgT.1", "S2", "S3"], tag="F.le1")
    for g in range(2):
        bank = tr_banks[g]
        bp = PS[bank]
        for c8 in range(g * 4, g * 4 + 4):
            em.op("pe", lambda e, c8=c8, g=g, bp=bp: e.transpose(
                bp[:, (c8 - g * 4) * 8:(c8 - g * 4 + 1) * 8], lgT[:, c8 * 128:(c8 + 1) * 128], idf[0:8, 0:8]),
                reads=[f"lgT.{c8 // 4}", "S0", "S1", "S2", "S3", "idf"], writes=[bank], tag=f"F.lt{c8}")
        em.op("dve", lambda e, bp=bp, g=g: e.tensor_copy(lgtm[:, g * 32:(g + 1) * 32], bp[:, 0:32]),
              reads=[bank], writes=[f"lgtm.{g}"], tag=f"F.lte{g}")
    for c8 in range(8):
        lg8 = lgtm[:, c8 * 8:(c8 + 1) * 8]
        em.op("dve", lambda e, lg8=lg8: e.reduce_max(mx1[:, :], lg8, AX.X),
              reads=[f"lgtm.{c8 // 4}"], writes=["mx1"], tag=f"F.m{c8}")
        em.op("dve", lambda e: e.tensor_scalar(mx1[:, :], mx1[:, :], -1.0, None, ALU.mult),
              reads=["mx1"], writes=["mx1"], tag=f"F.n{c8}")
        em.op("act", lambda e, lg8=lg8: e.activation(ee8[:, :], lg8, AF.Exp, bias=mx1[:, 0:1], scale=1.0),
              reads=[f"lgtm.{c8 // 4}", "mx1"], writes=["ee8"], tag=f"F.e{c8}")
        em.op("dve", lambda e: e.max(out=mx8[:, :], in_=ee8[:, :]),
              reads=["ee8"], writes=["mx8"], tag=f"F.x8{c8}")
        em.op("dve", lambda e: e.memset(scr8[:, :], 0.0), reads=[], writes=["scr8"], tag=f"F.z{c8}")
        em.op("dve", lambda e: e.tensor_copy(scr8[:, 0:2], mx8[:, 0:2]),
              reads=["mx8"], writes=["scr8"], tag=f"F.c2{c8}")
        em.op("dve", lambda e: e.match_replace(out=mr8[:, :], in_to_replace=scr8[:, :],
                                               in_values=ee8[:, :], imm_value=0.0),
              reads=["scr8", "ee8"], writes=["mr8"], tag=f"F.mr{c8}")
        em.op("dve", lambda e: e.tensor_sub(mr8[:, :], ee8[:, :], mr8[:, :]),
              reads=["ee8", "mr8"], writes=["mr8"], tag=f"F.k{c8}")
        em.op("dve", lambda e: e.tensor_tensor(den[:, :], mx8[:, 0:1], mx8[:, 1:2], ALU.add),
              reads=["mx8"], writes=["den"], tag=f"F.d{c8}")
        em.op("dve", lambda e: e.reciprocal(rden1[:, :], den[:, :]),
              reads=["den"], writes=["rden1"], tag=f"F.rd{c8}")
        em.op("dve", lambda e, c8=c8: e.tensor_scalar(cmbtm[:, c8 * 8:(c8 + 1) * 8], mr8[:, :],
                                                      rden1[:, 0:1], 2.0, ALU.mult, ALU.mult),
              reads=["mr8", "rden1"], writes=[f"cmbtm.{c8}"], tag=f"F.cm{c8}")
    for g in range(2):
        bank = tr_banks[2 + g]
        bp = PS[bank]
        for c8 in range(g * 4, g * 4 + 4):
            em.op("pe", lambda e, c8=c8, bp=bp: e.transpose(
                bp[0:8, (c8 % 4) * 128:(c8 % 4 + 1) * 128], cmbtm[:, c8 * 8:(c8 + 1) * 8], idf[:, :]),
                reads=[f"cmbtm.{c8}", "idf"], writes=[bank], tag=f"F.ct{c8}")
        em.op("dve", lambda e, bp=bp, g=g: e.tensor_copy(cmbT[:, g * 512:(g + 1) * 512], bp[0:8, 0:512]),
              reads=[bank, f"lgT.{g}"], writes=[f"cmbT.{g}", f"lgT.{g}", f"S{2*g}", f"S{2*g+1}"], tag=f"F.cte{g}")
    for nh, (bk, bp) in enumerate([("pE", psE), ("pF", psF)]):
        em.op("pe", lambda e, nh=nh, bp=bp: e.matmul(
            bp[:, 0:512], e8f[:, :], cmbT[0:8, nh * 512:(nh + 1) * 512], start=True, stop=True),
            reads=[f"cmbT.{nh}", f"S{2*nh}", f"S{2*nh+1}", "e8f"], writes=[bk], tag=f"F.ex{nh}")
        em.op("dve", lambda e, nh=nh, bp=bp, c128=c128: e.tensor_copy(
            c128[:, nh * 512:(nh + 1) * 512], bp[:, 0:512]),
            reads=[bk], writes=[f"c128.{nh}", f"S{8 + 2 * nh}", f"S{9 + 2 * nh}"], tag=f"F.exe{nh}")
    for c in range(NCH):
        for nh in range(2):
            em.op("pe", lambda e, c=c, nh=nh: e.matmul(
                psB[:, nh * 512:(nh + 1) * 512], aab[:, c * 128:(c + 1) * 128],
                BIGA[:, c * 1024 + nh * 512: c * 1024 + (nh + 1) * 512],
                start=(c == 0), stop=(c == 31)),
                reads=["aabld"] + RB(c * 128, (c + 1) * 128) + RA(c * 1024 + nh * 512, c * 1024 + (nh + 1) * 512),
                writes=[f"pB{nh}"], tag=f"F.h{c}.{nh}")
    for nh in range(2):
        em.op("dve", lambda e, nh=nh, c128=c128: e.tensor_tensor(
            hw_ap[:, nh * 512:(nh + 1) * 512], psB[:, nh * 512:(nh + 1) * 512],
            c128[:, nh * 512:(nh + 1) * 512], ALU.mult),
            reads=[f"pB{nh}", f"c128.{nh}"], writes=["S12", "S13"], tag=f"F.hw{nh}")

    spillG = big_matmul_spill(d_wb, d_raw2, "G", lora=True, npre=2)

    st_ids = []

    def x2_from_scratch(to, xt, xtr):
        rb = dma("sp", xt, d_x2[to], reads=xtr, writes=xtr, tag=f"H.x2{to}")
        em.ops[rb]["deps"].add(x2spills[to])

    def out_sink(to, outap, outres):
        st_ids.append(dma("st", d_out[to * 128:(to + 1) * 128, :], outap,
                          reads=outres, writes=[], tag=f"H.out{to}"))

    apply_phase(d_raw2, "H", False, spillG, x2_from_scratch, out_sink)

    fin = em.op("sp", lambda e: e.nop(), reads=[], writes=[], tag="fin")
    em.ops[fin]["deps"].update(st_ids)

    streams = em.build(sems, dsems)

    with stack, nc.Block() as block:
        @block.tensor
        def _(e):
            for f in streams["pe"]:
                f(e)

        @block.scalar
        def _(e):
            for f in streams["act"]:
                f(e)

        @block.vector
        def _(e):
            for f in streams["dve"]:
                f(e)

        @block.gpsimd
        def _(e):
            for f in streams["pool"]:
                f(e)

        @block.sync
        def _(e):
            for f in streams["sp"]:
                f(e)

    return nc



def _tile_weight(w):
    t = w.reshape(32, 128, 32, 128)
    t = np.ascontiguousarray(t.transpose(0, 3, 2, 1))
    return t.reshape(32, 128, 32 * 128)


def _tile_small(w):
    O = w.shape[0]
    t = w.reshape(O, 32, 128)
    t = np.ascontiguousarray(t.transpose(2, 1, 0))
    return t.reshape(128, 32 * O)


def _rope_tables(lo, n):
    inv = 1.0 / (10000.0 ** (np.arange(0, 128, 2, dtype=np.float32) / 128.0))
    t = np.arange(lo, lo + n, dtype=np.float32)
    f = np.outer(t, inv)
    cos = np.zeros((n, 128), dtype=np.float32)
    sin = np.zeros((n, 128), dtype=np.float32)
    cos[:, 0::2] = np.cos(f)
    cos[:, 1::2] = np.cos(f)
    sin[:, 0::2] = np.sin(f)
    sin[:, 1::2] = np.sin(f)
    sin_eff = sin.copy()
    sin_eff[:, 0:64] = -sin_eff[:, 0:64]
    return np.ascontiguousarray(cos.T), np.ascontiguousarray(sin_eff.T)


def kernel(L0_attn_x, L0_attn_Wq, L0_attn_Wk, L0_attn_Wv, L0_attn_Wo,
           L0_ffn_W_base, L0_ffn_b_base, L0_ffn_W_router, L0_ffn_A, L0_ffn_B):
    x = np.asarray(L0_attn_x, dtype=np.float32)
    wq_l = _tile_weight(np.asarray(L0_attn_Wq, dtype=np.float32))
    wo_l = _tile_weight(np.asarray(L0_attn_Wo, dtype=np.float32))
    wb_l = _tile_weight(np.asarray(L0_ffn_W_base, dtype=np.float32))
    wk_l = _tile_small(np.asarray(L0_attn_Wk, dtype=np.float32))
    wv_l = _tile_small(np.asarray(L0_attn_Wv, dtype=np.float32))
    wr_l = _tile_small(np.asarray(L0_ffn_W_router, dtype=np.float32))
    A = np.asarray(L0_ffn_A, dtype=np.float32)
    Bm = np.asarray(L0_ffn_B, dtype=np.float32)
    aa_l = np.ascontiguousarray(A.reshape(128, 32, 128).transpose(2, 1, 0)).reshape(128, 4096)
    ba_l = np.ascontiguousarray(Bm.transpose(0, 2, 1)).reshape(128, 4096)
    bb_l = np.ascontiguousarray(np.asarray(L0_ffn_b_base, dtype=np.float32).reshape(32, 128).T)

    alpha = np.linspace(0.0, 1.0, 256, dtype=np.float32)
    idb = np.eye(128, dtype=NPBF16)
    idf = np.eye(128, dtype=np.float32)
    onesc = np.ones((128, 128), dtype=NPBF16)
    o1c = np.ones((1, 128), dtype=NPBF16)
    e8 = np.zeros((8, 128), dtype=np.float32)
    for ei in range(8):
        e8[ei, ei * 16:(ei + 1) * 16] = 1.0

    in_maps = []
    for c in range(8):
        b, half = c // 2, c % 2
        qstart = half * 1024
        lo = qstart - 256
        xs = np.zeros((NSH, D), dtype=np.float32)
        glo, ghi = max(0, lo), min(S, lo + NSH)
        xs[glo - lo: ghi - lo] = x[b, glo:ghi]
        cosT, sinT = _rope_tables(lo, NSH)
        wv = np.zeros((128, 1536), dtype=np.float32)
        wv[:, 0:256] = 0.0 if half == 0 else (1.0 - alpha)
        wv[:, 256:512] = 1.0 if half == 0 else alpha
        wv[:, 512:768] = 1.0 - alpha
        wv[:, 768:1024] = alpha
        wv[:, 1024:1280] = 1.0 - alpha
        wv[:, 1280:1536] = alpha
        padc = np.zeros((1, 512), dtype=NPBF16)
        if half == 1:
            padc[0, 0:256] = NPBF16(-128.0)
        in_maps.append({
            "x_sh": xs, "wq_l": wq_l, "wo_l": wo_l, "wb_l": wb_l,
            "wk_l": wk_l, "wv_l": wv_l, "wr_l": wr_l,
            "aa_l": aa_l, "ba_l": ba_l, "bb_l": bb_l,
            "cos_l": cosT, "sin_l": sinT, "wv_c": wv, "pad_c": padc,
            "idb_c": idb, "idf_c": idf, "ones_c": onesc, "o1_c": o1c, "e8_c": e8,
        })

    nc = build_program()
    res = run_bass_kernel_spmd(nc, in_maps, core_ids=list(range(8)))
    out = np.empty((B_, S, D), dtype=np.float32)
    for c in range(8):
        b, half = c // 2, c % 2
        out[b, half * 1024:(half + 1) * 1024] = res.results[c]["out_sh"]
    return out


# version 37
# speedup vs baseline: 1.0104x; 1.0104x over previous
import numpy as np
import ml_dtypes
from contextlib import ExitStack

import concourse.bass as bass
import concourse.mybir as mybir
from concourse.bass_utils import run_bass_kernel_spmd

F32 = mybir.dt.float32
BF16 = mybir.dt.bfloat16
AF = mybir.ActivationFunctionType
ALU = mybir.AluOpType
AX = mybir.AxisListType
NPBF16 = ml_dtypes.bfloat16

D = 4096
NCH = 32
B_ = 4
S = 2048
NTOK = 1024
NSH = 1536
NT_SH = 12
NT_OWN = 8
QOFF = 256
H = 32
EPS = 1e-5
SCALE = float(1.0 / np.sqrt(128.0))

SAME_ENGINE_SYNC = True

ENG_COMPUTE = ("pe", "act", "dve", "pool", "sp")


class Em:

    def __init__(self):
        self.ops = []
        self.res = {}

    def op(self, engine, fn, reads=(), writes=(), dma=None, tag=""):
        opid = len(self.ops)
        deps = set()
        for r in reads:
            st = self.res.get(r)
            if st is not None and st["w"] is not None:
                deps.add(st["w"])
        for w in writes:
            st = self.res.get(w)
            if st is not None:
                if st["w"] is not None:
                    deps.add(st["w"])
                deps.update(st["r"])
        for r in reads:
            self.res.setdefault(r, {"w": None, "r": []})["r"].append(opid)
        for w in writes:
            self.res[w] = {"w": opid, "r": []}
        deps.discard(opid)
        self.ops.append(dict(engine=engine, fn=fn, deps=deps, dma=dma, tag=tag))
        return opid

    def build(self, sems, dsems):
        needed = set()
        for op in self.ops:
            needed |= op["deps"]
        cnt = {e: 0 for e in ENG_COMPUTE}
        dcnt = {q: 0 for q in dsems}
        meta = []
        for i, op in enumerate(self.ops):
            if op["dma"] is not None:
                dcnt[op["dma"]] += 1
                meta.append(("dma", op["dma"], dcnt[op["dma"]]))
            elif i in needed:
                cnt[op["engine"]] += 1
                meta.append(("mark", op["engine"], cnt[op["engine"]]))
            else:
                meta.append(("plain", op["engine"], cnt[op["engine"]]))
        streams = {e: [] for e in ENG_COMPUTE}
        watermark = {e: {} for e in ENG_COMPUTE}
        for i, op in enumerate(self.ops):
            eng = op["engine"]
            waits = {}
            for d in op["deps"]:
                kind, key, val = meta[d]
                if kind == "dma":
                    sk = ("dma", key)
                    waits[sk] = max(waits.get(sk, 0), 16 * val)
                else:
                    deng = self.ops[d]["engine"]
                    if deng == eng and (not SAME_ENGINE_SYNC or deng == "pe"):
                        continue
                    if kind == "plain":
                        raise RuntimeError(f"dep on unmarked op {d} ({self.ops[d]['tag']}) from {i} ({op['tag']})")
                    sk = ("eng", key)
                    waits[sk] = max(waits.get(sk, 0), val)
            wm = watermark[eng]
            emit_waits = []
            for sk, val in sorted(waits.items()):
                if wm.get(sk, 0) < val:
                    wm[sk] = val
                    emit_waits.append((sk, val))
            kind, key, val = meta[i]

            def make(emit_waits=emit_waits, op=op, kind=kind, key=key):
                def run(eh):
                    for (skind, sname), v in emit_waits:
                        sem = dsems[sname] if skind == "dma" else sems[sname]
                        eh.wait_ge(sem, v)
                    inst = op["fn"](eh)
                    if kind == "dma":
                        inst.then_inc(dsems[key], 16)
                    elif kind == "mark":
                        inst.then_inc(sems[key], 1)
                return run
            streams[eng].append(make())
        return streams


def build_program():
    nc = bass.Bass("TRN2", target_bir_lowering=False)

    d_x = nc.dram_tensor("x_sh", [NSH, D], F32, kind="ExternalInput")
    d_wq = nc.dram_tensor("wq_l", [NCH, 128, NCH * 128], F32, kind="ExternalInput")
    d_wo = nc.dram_tensor("wo_l", [NCH, 128, NCH * 128], F32, kind="ExternalInput")
    d_wb = nc.dram_tensor("wb_l", [NCH, 128, NCH * 128], F32, kind="ExternalInput")
    d_wk = nc.dram_tensor("wk_l", [128, NCH * 128], F32, kind="ExternalInput")
    d_wv = nc.dram_tensor("wv_l", [128, NCH * 128], F32, kind="ExternalInput")
    d_wr = nc.dram_tensor("wr_l", [128, NCH * 8], F32, kind="ExternalInput")
    d_aa = nc.dram_tensor("aa_l", [128, NCH * 128], F32, kind="ExternalInput")
    d_ba = nc.dram_tensor("ba_l", [128, NCH * 128], F32, kind="ExternalInput")
    d_bb = nc.dram_tensor("bb_l", [128, NCH], F32, kind="ExternalInput")
    d_cos = nc.dram_tensor("cos_l", [128, NSH], F32, kind="ExternalInput")
    d_sin = nc.dram_tensor("sin_l", [128, NSH], F32, kind="ExternalInput")
    d_wvv = nc.dram_tensor("wv_c", [128, 1536], F32, kind="ExternalInput")
    d_pad = nc.dram_tensor("pad_c", [1, 1024], BF16, kind="ExternalInput")
    d_idb = nc.dram_tensor("idb_c", [128, 128], BF16, kind="ExternalInput")
    d_idf = nc.dram_tensor("idf_c", [128, 128], F32, kind="ExternalInput")
    d_ones = nc.dram_tensor("ones_c", [128, 128], BF16, kind="ExternalInput")
    d_o1 = nc.dram_tensor("o1_c", [1, 128], BF16, kind="ExternalInput")
    d_e8 = nc.dram_tensor("e8_c", [8, 128], F32, kind="ExternalInput")

    d_out = nc.dram_tensor("out_sh", [NTOK, D], F32, kind="ExternalOutput")

    d_raw1 = nc.dram_tensor("raw1_d", [128, NT_OWN, NCH, 128], F32)
    d_raw2 = nc.dram_tensor("raw2_d", [128, NT_OWN, NCH, 128], F32)
    d_x2 = nc.dram_tensor("x2_d", [NT_OWN, 128, D], F32)

    stack = ExitStack()
    sb = lambda nm, sh, dt=F32: stack.enter_context(nc.sbuf_tensor(nm, sh, dt))

    cosT = sb("cosT", [128, NSH])
    sinT = sb("sinT", [128, NSH])
    idb = sb("idb", [128, 128], BF16)
    idf = sb("idf", [128, 128])
    onesb = sb("onesb", [128, 128], BF16)
    o1b = sb("o1b", [1, 128], BF16)
    e8f = sb("e8f", [8, 128])
    wvc = sb("wvc", [128, 1536])
    padr = sb("padr", [1, 1024], BF16)
    wrb = sb("wrb", [128, NCH * 8], BF16)
    biasl = sb("biasl", [128, NCH])
    wbuf = [sb("wbuf0", [128, NCH * 128], BF16), sb("wbuf1", [128, NCH * 128], BF16)]
    BIGA = sb("BIGA", [128, 32 * 1024], BF16)
    BIGB = sb("BIGB", [128, 32 * 1024], BF16)
    SCR = sb("SCR", [128, 8 * 1024], BF16)
    kr = sb("kr", [128, NSH], BF16)
    vt = sb("vt", [128, NSH], BF16)
    bn6_ = [sb("bn6a", [128, 48]), sb("bn6b", [128, 48])]
    mv_ = [sb("mva", [128, 2]), sb("mvb", [128, 2])]
    sqv_ = [sb("sqva", [128, 1]), sb("sqvb", [128, 1])]
    rstd_ = [sb("rstda", [128, 1]), sb("rstdb", [128, 1])]
    nbias_ = [sb("nbiasa", [128, 1]), sb("nbiasb", [128, 1])]
    tmpA = sb("tmpA", [128, 256])
    csb = [sb("csb0", [128, 512]), sb("csb1", [128, 512]), sb("csb2", [128, 512])]
    tails = [sb(f"tail{k}", [128, 256]) for k in range(4)]
    lgT = SCR[0:8, 0:2048].bitcast(F32)
    cmbT = lgT
    lgtm = sb("lgtm", [128, 64])
    cmbtm = sb("cmbtm", [128, 64])
    mx1 = sb("mx1", [128, 1])
    mx8 = sb("mx8", [128, 8])
    scr8 = sb("scr8", [128, 8])
    mr8 = sb("mr8", [128, 8])
    ee8 = sb("ee8", [128, 8])
    den = sb("den", [128, 1])
    rden1 = sb("rden1", [128, 1])
    epsv = sb("epsv", [128, 1])

    psA = stack.enter_context(nc.psum_tensor("psA", [128, 1024], F32))
    psB = stack.enter_context(nc.psum_tensor("psB", [128, 1024], F32))
    psC = stack.enter_context(nc.psum_tensor("psC", [128, 512], F32))
    psD = stack.enter_context(nc.psum_tensor("psD", [128, 512], F32))
    psE = stack.enter_context(nc.psum_tensor("psE", [128, 512], F32))
    psF = stack.enter_context(nc.psum_tensor("psF", [128, 512], F32))

    sems = {e: stack.enter_context(nc.semaphore(f"s_{e}")) for e in ENG_COMPUTE}
    dsems = {"sp": stack.enter_context(nc.semaphore("d_sp")),
             "pool": stack.enter_context(nc.semaphore("d_pool")),
             "actq": stack.enter_context(nc.semaphore("d_actq")),
             "st": stack.enter_context(nc.semaphore("d_st"))}

    em = Em()

    def _regs(pref, lo, hi, g):
        return [f"{pref}{k}" for k in range(lo // g, (hi + g - 1) // g)]

    def RA(lo, hi):
        return _regs("A", lo, hi, 512)

    def RB(lo, hi):
        return _regs("B", lo, hi, 512)

    def RS(lo, hi):
        return _regs("S", lo, hi, 512)

    def bigb_f32(col_f32, n_f32):
        return BIGB[:, 2 * col_f32: 2 * (col_f32 + n_f32)].bitcast(F32)

    def scr_f32(col_f32, n_f32):
        return SCR[:, 2 * col_f32: 2 * (col_f32 + n_f32)].bitcast(F32)

    PS = {"pA0": psA[:, 0:512], "pA1": psA[:, 512:1024],
          "pB0": psB[:, 0:512], "pB1": psB[:, 512:1024],
          "pC": psC[:, :], "pD": psD[:, :], "pE": psE[:, :], "pF": psF[:, :]}

    def dma(queue, dst, src, reads=(), writes=(), tag=""):
        host = {"sp": "sp", "pool": "pool", "actq": "act", "st": "sp"}[queue]
        return em.op(host, lambda e, d=dst, s=src: e.dma_start(d, s),
                     reads=reads, writes=writes, dma=queue, tag=tag)

    def w_prefetch(d_w, j, tagp):
        wres = [f"wb{j % 2}"]
        dma("pool", wbuf[j % 2][:, :], d_w[j, :, :], reads=wres, writes=wres, tag=f"{tagp}.w{j}")

    dma("sp", cosT[:, :], d_cos[:, :], writes=["cosT"])
    dma("sp", sinT[:, :], d_sin[:, :], writes=["sinT"])
    dma("sp", idb[:, :], d_idb[:, :], writes=["idb"])
    dma("sp", idf[:, :], d_idf[:, :], writes=["idf"])
    dma("sp", onesb[:, :], d_ones[:, :], writes=["onesb"])
    dma("sp", o1b[:, :], d_o1[:, :], writes=["o1b"])
    dma("sp", e8f[:, :], d_e8[:, :], writes=["e8f"])
    dma("sp", wvc[:, :], d_wvv[:, :], writes=["wvc"])
    dma("sp", padr[:, :], d_pad[:, :], writes=["padr"])
    dma("sp", biasl[:, :], d_bb[:, :], writes=["biasl"])
    em.op("dve", lambda e: e.memset(epsv[:, :], float(EPS)), reads=[], writes=["epsv"], tag="init.eps")
    dma("pool", wrb[:, :], d_wr[:, :], writes=["wrb"])
    dma("pool", wbuf[0][:, :], d_wk[:, :], writes=["wb0"])
    dma("pool", wbuf[1][:, :], d_wv[:, :], writes=["wb1"])

    def emit_bn_chunk(src_ap, src_res, par, cc, tagp):
        bn6 = bn6_[par]
        sl = src_ap if src_ap.shape[-1] == 512 else src_ap[:, cc * 512:(cc + 1) * 512]
        em.op("dve", lambda e, cc=cc, sl=sl, bn6=bn6: e.bn_stats(
            bn6[:, cc * 6:(cc + 1) * 6], sl),
            reads=src_res, writes=[f"bn6{par}.{cc}"], tag=f"{tagp}.bn{cc}")

    def emit_ln_finish(par, tagp):
        bn6, mv, sqv, rstd, nbias = bn6_[par], mv_[par], sqv_[par], rstd_[par], nbias_[par]
        em.op("dve", lambda e, bn6=bn6, mv=mv: e.bn_aggr(mv[:, :], bn6.rearrange("p (c s) -> p c s", s=6)),
              reads=[f"bn6{par}.{cc}" for cc in range(8)], writes=[f"mv{par}"], tag=f"{tagp}.bnagg")
        em.op("act", lambda e, mv=mv, sqv=sqv: e.activation(sqv[:, :], mv[:, 1:2], AF.Sqrt, bias=epsv[:, 0:1], scale=1.0),
              reads=[f"mv{par}", "epsv"], writes=[f"sqv{par}"], tag=f"{tagp}.sqrt")
        em.op("dve", lambda e, sqv=sqv, rstd=rstd: e.reciprocal(rstd[:, :], sqv[:, :]),
              reads=[f"sqv{par}"], writes=[f"rstd{par}"], tag=f"{tagp}.rcp")
        em.op("dve", lambda e, mv=mv, rstd=rstd, nbias=nbias: e.tensor_scalar(
            nbias[:, :], mv[:, 0:1], rstd[:, 0:1], -1.0, ALU.mult, ALU.mult),
            reads=[f"mv{par}", f"rstd{par}"], writes=[f"nbias{par}"], tag=f"{tagp}.nb")

    def emit_ln_stats(src_ap, src_res, tagp, par=0):
        for cc in range(8):
            emit_bn_chunk(src_ap, src_res, par, cc, tagp)
        emit_ln_finish(par, tagp)

    tr_banks = ["pC", "pD", "pE", "pF"]

    def xslot(i):
        return bigb_f32(4096 * (i % 2), 4096)

    def xslot_cols(i):
        return (8192 * (i % 2), 8192 * (i % 2) + 8192)

    hxF = BIGB[:, 16384:24576]
    hxB = BIGB[:, 24576:32768]

    OWN = list(range(2, 10))

    def ln1t_buf(i):
        return SCR[:, 4096 * (i % 2): 4096 * (i % 2) + 4096]

    def ln1t_cols(i):
        return (4096 * (i % 2), 4096 * (i % 2) + 4096)

    def s1_dma(t):
        xr = RB(*xslot_cols(t))
        dma("sp", xslot(t), d_x[t * 128:(t + 1) * 128, :], reads=xr, writes=xr, tag=f"A.x{t}")

    def s1_stats(t):
        xr = RB(*xslot_cols(t))
        emit_ln_stats(xslot(t), xr, f"A.{t}", par=t % 2)
        lr = RS(*ln1t_cols(t))
        em.op("act", lambda e, t=t: e.activation(ln1t_buf(t), xslot(t), AF.Identity,
                                                 bias=nbias_[t % 2][:, 0:1], scale=rstd_[t % 2][:, 0:1]),
              reads=xr + [f"rstd{t % 2}", f"nbias{t % 2}"], writes=lr, tag=f"A.ap{t}")

    def s1_tr(t):
        lr = RS(*ln1t_cols(t))
        for cg in range(0, 32, 8):
            bank = tr_banks[(cg // 8) % 4]
            bview = PS[bank].bitcast(BF16)
            for c in range(cg, cg + 8):
                em.op("pe", lambda e, c=c, cg=cg, t=t, bv=bview: e.transpose(
                    bv[:, (c - cg) * 128:(c - cg + 1) * 128],
                    ln1t_buf(t)[:, c * 128:(c + 1) * 128], idb[:, :]),
                    reads=lr + ["idb"], writes=[bank], tag=f"A.tr{t}.{c}")
            if t in OWN:
                to = t - 2
                out_ap = BIGA.rearrange("p (c n) -> p c n", n=1024)[:, cg:cg + 8, to * 128:(to + 1) * 128]
                wres = RA(cg * 1024, (cg + 8) * 1024)
            else:
                hx = hxF if t < 2 else hxB
                base = 16384 if t < 2 else 24576
                ti = t % 2
                out_ap = hx.rearrange("p (c n) -> p c n", n=256)[:, cg:cg + 8, ti * 128:(ti + 1) * 128]
                wres = RB(base + cg * 256, base + (cg + 8) * 256)
            em.op("dve", lambda e, bv=bview, oa=out_ap: e.tensor_copy(
                oa, bv.rearrange("p (c n) -> p c n", n=128)),
                reads=[bank], writes=wres, tag=f"A.trev{t}.{cg}")

    s1_dma(OWN[0])
    tpend = None
    for j, t in enumerate(OWN):
        if j + 1 < len(OWN):
            s1_dma(OWN[j + 1])
        s1_stats(t)
        if tpend is not None:
            s1_tr(tpend)
        tpend = t
    s1_tr(tpend)

    KVG = [
        (lambda c: hxF[:, c * 256:(c + 1) * 256], lambda c: RB(16384 + c * 256, 16384 + (c + 1) * 256), 0, 256, [0, 1]),
        (lambda c: BIGA[:, c * 1024 + 0: c * 1024 + 512], lambda c: RA(c * 1024, c * 1024 + 512), 256, 512, [2, 3, 4, 5]),
        (lambda c: BIGA[:, c * 1024 + 512: c * 1024 + 1024], lambda c: RA(c * 1024 + 512, c * 1024 + 1024), 768, 512, [6, 7, 8, 9]),
        (lambda c: hxB[:, c * 256:(c + 1) * 256], lambda c: RB(24576 + c * 256, 24576 + (c + 1) * 256), 1280, 256, [10, 11]),
    ]
    vsb5 = csb[2][:, :].bitcast(BF16)[:, 0:512]

    def kv_k(gi):
        (rhs_fn, rres_fn, klo, n, ktiles) = KVG[gi]
        krres = [f"kr{t}" for t in ktiles]
        for c in range(NCH):
            em.op("pe", lambda e, c=c, rhs_fn=rhs_fn, n=n: e.matmul(
                psC[:, 0:n], wbuf[0][:, c * 128:(c + 1) * 128], rhs_fn(c),
                start=(c == 0), stop=(c == 31)),
                reads=["wb0"] + rres_fn(c), writes=["pC"], tag=f"A.k{gi}.{c}")
        qc = csb[0][:, 0:n]
        m2 = csb[1][:, 0:n]
        cs = cosT[:, klo:klo + n]
        sn = sinT[:, klo:klo + n]
        em.op("dve", lambda e, cs=cs, qc=qc, n=n: e.tensor_tensor(qc, psC[:, 0:n], cs, ALU.mult),
              reads=["pC", "cosT"], writes=["csb0"], tag=f"A.rk1.{gi}")
        em.op("dve", lambda e, sn=sn, m2=m2, n=n: e.tensor_tensor(m2[0:64, :], psC[64:128, 0:n], sn[0:64, :], ALU.mult),
              reads=["pC", "sinT"], writes=["csb1"], tag=f"A.rk2.{gi}")
        em.op("dve", lambda e, sn=sn, m2=m2, n=n: e.tensor_tensor(m2[64:128, :], psC[0:64, 0:n], sn[64:128, :], ALU.mult),
              reads=["pC", "sinT"], writes=["csb1b"], tag=f"A.rk3.{gi}")
        em.op("dve", lambda e, klo=klo, n=n, qc=qc, m2=m2: e.tensor_tensor(kr[:, klo:klo + n], qc, m2, ALU.add),
              reads=["csb0", "csb1", "csb1b"], writes=krres, tag=f"A.rk4.{gi}")

    def kv_v(gi):
        (rhs_fn, rres_fn, klo, n, ktiles) = KVG[gi]
        vtres = [f"vt{t}" for t in ktiles]
        for c in range(NCH):
            em.op("pe", lambda e, c=c, rhs_fn=rhs_fn, n=n: e.matmul(
                psD[:, 0:n], wbuf[1][:, c * 128:(c + 1) * 128], rhs_fn(c),
                start=(c == 0), stop=(c == 31)),
                reads=["wb1"] + rres_fn(c), writes=["pD"], tag=f"A.v{gi}.{c}")
        em.op("act", lambda e, n=n: e.copy(vsb5[:, 0:n], psD[:, 0:n]),
              reads=["pD"], writes=["csb2"], tag=f"A.vc{gi}")
        for k in range(n // 128):
            bank = ["pE", "pF"][k % 2]
            bview = PS[bank].bitcast(BF16)
            em.op("pe", lambda e, k=k, bv=bview: e.transpose(
                bv[:, 0:128], vsb5[:, k * 128:(k + 1) * 128], idb[:, :]),
                reads=["csb2", "idb"], writes=[bank], tag=f"A.vt{gi}.{k}")
            em.op("dve", lambda e, k=k, klo=klo, bv=bview: e.tensor_copy(
                vt[:, klo + k * 128: klo + (k + 1) * 128], bv[:, 0:128]),
                reads=[bank], writes=vtres, tag=f"A.vte{gi}.{k}")

    s1_dma(0)
    s1_dma(1)
    kv_k(1)
    s1_stats(0)
    s1_stats(1)
    kv_k(2)
    s1_tr(0)
    s1_tr(1)
    s1_dma(10)
    s1_dma(11)
    s1_stats(10)
    s1_stats(11)
    s1_tr(10)
    s1_tr(11)
    kv_k(0)
    kv_k(3)
    kv_v(1)
    kv_v(2)
    kv_v(0)
    kv_v(3)

    def qr_head(h):
        return BIGB[:, h * 1024:(h + 1) * 1024]

    for h in range(H):
        wres = [f"wb{h % 2}"]
        dma("pool", wbuf[h % 2][:, :], d_wq[h, :, :], reads=wres, writes=wres, tag=f"B.w{h}")
        pX = psA if h % 2 == 0 else psB
        pXr = ["pA0", "pA1"] if h % 2 == 0 else ["pB0", "pB1"]
        for c in range(NCH):
            for nh in range(2):
                em.op("pe", lambda e, c=c, nh=nh, h=h, pX=pX: e.matmul(
                    pX[:, nh * 512:(nh + 1) * 512], wbuf[h % 2][:, c * 128:(c + 1) * 128],
                    BIGA[:, c * 1024 + nh * 512: c * 1024 + (nh + 1) * 512],
                    start=(c == 0), stop=(c == 31)),
                    reads=wres + RA(c * 1024 + nh * 512, c * 1024 + (nh + 1) * 512),
                    writes=[pXr[nh]], tag=f"B.q{h}.{c}.{nh}")
        for nh in range(2):
            qc = csb[0][:, :]
            m2 = csb[1][:, :]
            cs = cosT[:, QOFF + nh * 512: QOFF + (nh + 1) * 512]
            sn = sinT[:, QOFF + nh * 512: QOFF + (nh + 1) * 512]
            pslice = pX[:, nh * 512:(nh + 1) * 512]
            qres = RB(h * 1024 + nh * 512, h * 1024 + (nh + 1) * 512)
            em.op("dve", lambda e, pslice=pslice, cs=cs, qc=qc: e.tensor_tensor(qc, pslice, cs, ALU.mult),
                  reads=[pXr[nh], "cosT"], writes=["csb0"], tag=f"B.r1.{h}.{nh}")
            em.op("dve", lambda e, pslice=pslice, sn=sn, m2=m2: e.tensor_tensor(
                m2[0:64, :], pslice[64:128, :], sn[0:64, :], ALU.mult),
                reads=[pXr[nh], "sinT"], writes=["csb1"], tag=f"B.r2.{h}.{nh}")
            em.op("dve", lambda e, pslice=pslice, sn=sn, m2=m2: e.tensor_tensor(
                m2[64:128, :], pslice[0:64, :], sn[64:128, :], ALU.mult),
                reads=[pXr[nh], "sinT"], writes=["csb1b"], tag=f"B.r3.{h}.{nh}")
            em.op("dve", lambda e, h=h, nh=nh, qc=qc, m2=m2: e.tensor_tensor(
                qr_head(h)[:, nh * 512:(nh + 1) * 512], qc, m2, ALU.add),
                reads=["csb0", "csb1", "csb1b"] + qres, writes=qres, tag=f"B.r4.{h}.{nh}")

    WIN = []
    for i in range(5):
        qlo = max(0, 256 * i - 256)
        qhi = min(NTOK, 256 * i + 256)
        Areg = (256 * i - 256, 256 * i) if i >= 1 else None
        Breg = (256 * i, 256 * i + 256) if i <= 3 else None
        WIN.append((qlo, qhi - qlo, Areg, Breg))

    def attnT_buf(par, kt):
        base = 4096 + par * 2048
        return SCR[:, base + kt * 512: base + (kt + 1) * 512]

    def attnT_res(par, kt):
        return [f"S{8 + par * 4 + kt}"]

    sc_banks = ["pC", "pD", "pA0", "pA1"]
    sc_ps = [psC[:, :], psD[:, :], psA[:, 0:512], psA[:, 512:1024]]

    sm_banks = [("pE", psE[:, 0:512]), ("pB0", psB[:, 0:512]), ("pA0", psA[:, 0:512])]
    av_banks = [("pF", psF[:, 0:512]), ("pB1", psB[:, 512:1024]), ("pA1", psA[:, 512:1024])]
    sc_bk = [("pC", psC[:, :]), ("pD", psD[:, :])]

    def avsb_buf(k):
        return scr_f32(512 * k, 512)

    def avsb_res(k):
        return [f"S{2 * k}", f"S{2 * k + 1}"]

    def post_pe(p):
        (h, i, par3, par2, qlo, qcnt, Areg, Breg) = p
        smbk, smps = sm_banks[par3]
        avbk, avps = av_banks[par3]
        for kt in range(4):
            ktile = (2 * i + kt) * 128
            em.op("pe", lambda e, par2=par2, kt=kt, qcnt=qcnt, smps=smps, i=i: e.matmul(
                smps[:, 0:qcnt], onesb[:, :], attnT_buf(par2, kt)[:, 0:qcnt],
                start=(kt == 0), stop=(kt == 3 and i != 4)),
                reads=attnT_res(par2, kt) + ["onesb"], writes=[smbk], tag=f"C.sm{h}.{i}.{kt}")
            em.op("pe", lambda e, par2=par2, kt=kt, ktile=ktile, qcnt=qcnt, avps=avps: e.matmul(
                avps[:, 0:qcnt], vt[:, ktile:ktile + 128], attnT_buf(par2, kt)[:, 0:qcnt],
                start=(kt == 0), stop=(kt == 3)),
                reads=attnT_res(par2, kt) + [f"vt{ktile // 128}"], writes=[avbk], tag=f"C.av{h}.{i}.{kt}")
        if i == 4:
            em.op("pe", lambda e, qcnt=qcnt, smps=smps: e.matmul(
                smps[:, 0:qcnt], o1b[:, :], padr[0:1, 512:512 + qcnt],
                start=False, stop=True),
                reads=["padr", "o1b"], writes=[smbk], tag=f"C.pd{h}.{i}")

    def post_tail(p):
        (h, i, par3, par2, qlo, qcnt, Areg, Breg) = p
        smbk, smps = sm_banks[par3]
        avbk, avps = av_banks[par3]
        cb = csb[par3]
        cbr = f"csb{par3}"
        wcl = [cbr, "csb1b"] if par3 == 1 else [cbr]
        wvo = [0, 256, 768, 768, 1280][i]
        em.op("dve", lambda e, cb=cb, qcnt=qcnt, smps=smps: e.reciprocal(cb[:, 0:qcnt], smps[:, 0:qcnt]),
              reads=[smbk], writes=wcl, tag=f"C.rc{h}.{i}")
        cweng = "dve" if (5 * h + i) % 4 == 3 else "pool"
        em.op(cweng, lambda e, cb=cb, qcnt=qcnt, wvo=wvo: e.tensor_tensor(
            cb[:, 0:qcnt], cb[:, 0:qcnt], wvc[:, wvo:wvo + qcnt], ALU.mult),
            reads=[cbr, "wvc"], writes=[cbr], tag=f"C.cw{h}.{i}")
        avs = avsb_buf(par3)
        em.op("act", lambda e, avs=avs, qcnt=qcnt, avps=avps: e.copy(avs[:, 0:qcnt], avps[:, 0:qcnt]),
              reads=[avbk], writes=avsb_res(par3), tag=f"C.ave{h}.{i}")
        tl = tails[h % 4]
        tlr = f"tail{h % 4}"
        if Areg is not None:
            aoff = Areg[0] - qlo
            ares = RA(h * 1024 + Areg[0], h * 1024 + Areg[1])
            em.op("pool", lambda e, cb=cb, aoff=aoff, avs=avs: e.tensor_tensor(
                cb[:, aoff:aoff + 256], cb[:, aoff:aoff + 256], avs[:, aoff:aoff + 256], ALU.mult),
                reads=[cbr] + avsb_res(par3), writes=[cbr], tag=f"C.t1{h}.{i}")
            em.op("pool", lambda e, h=h, Areg=Areg, cb=cb, aoff=aoff, tl=tl: e.tensor_tensor(
                BIGA[:, h * 1024 + Areg[0]: h * 1024 + Areg[1]], cb[:, aoff:aoff + 256], tl[:, :], ALU.add),
                reads=[cbr, tlr] + ares, writes=ares, tag=f"C.bA{h}.{i}")
        if Breg is not None:
            boff = Breg[0] - qlo
            em.op("pool", lambda e, cb=cb, boff=boff, avs=avs, tl=tl: e.tensor_tensor(
                tl[:, :], avs[:, boff:boff + 256], cb[:, boff:boff + 256], ALU.mult),
                reads=[cbr] + avsb_res(par3), writes=[tlr], tag=f"C.bB{h}.{i}")

    w_prefetch(d_wo, 0, "D")
    w_prefetch(d_wo, 1, "D")
    pend = None
    tailq = []
    idx = 0
    for h in range(H):
        for i, (qlo, qcnt, Areg, Breg) in enumerate(WIN):
            par3 = idx % 3
            par2 = idx % 2
            for kt in range(4):
                bk, bp = sc_bk[kt % 2]
                ktile = (2 * i + kt) * 128
                em.op("pe", lambda e, bp=bp, ktile=ktile, h=h, qlo=qlo, qcnt=qcnt: e.matmul(
                    bp[:, 0:qcnt], kr[:, ktile:ktile + 128],
                    qr_head(h)[:, qlo:qlo + qcnt], start=True, stop=True),
                    reads=[f"kr{ktile // 128}"] + RB(h * 1024 + qlo, h * 1024 + qlo + qcnt),
                    writes=[bk], tag=f"C.sc{h}.{i}.{kt}")
                em.op("act", lambda e, bp=bp, par2=par2, kt=kt, qcnt=qcnt: e.activation(
                    attnT_buf(par2, kt)[:, 0:qcnt], bp[:, 0:qcnt], AF.Sigmoid, scale=SCALE),
                    reads=[bk], writes=attnT_res(par2, kt), tag=f"C.sg{h}.{i}.{kt}")
            if pend is not None:
                post_pe(pend)
                tailq.append(pend)
                if len(tailq) == 2:
                    post_tail(tailq.pop(0))
            pend = (h, i, par3, par2, qlo, qcnt, Areg, Breg)
            idx += 1
    post_pe(pend)
    tailq.append(pend)
    for t in tailq:
        post_tail(t)

    def slab_buf(par):
        return scr_f32(par * 1024, 1024)

    def slab_res(par):
        return [f"S{par}"]

    def big_matmul_spill(d_w, d_raw, tagp, lora=False, npre=0):
        spills = []
        for j in range(NCH):
            wres = [f"wb{j % 2}"]
            if j >= npre:
                dma("pool", wbuf[j % 2][:, :], d_w[j, :, :], reads=wres, writes=wres, tag=f"{tagp}.w{j}")
            pX = psA if j % 2 == 0 else psB
            pXr = ["pA0", "pA1"] if j % 2 == 0 else ["pB0", "pB1"]
            for c in range(NCH):
                for nh in range(2):
                    em.op("pe", lambda e, c=c, nh=nh, j=j, pX=pX: e.matmul(
                        pX[:, nh * 512:(nh + 1) * 512], wbuf[j % 2][:, c * 128:(c + 1) * 128],
                        BIGA[:, c * 1024 + nh * 512: c * 1024 + (nh + 1) * 512],
                        start=(c == 0), stop=(c == 31 and not lora)),
                        reads=wres + RA(c * 1024 + nh * 512, c * 1024 + (nh + 1) * 512),
                        writes=[pXr[nh]], tag=f"{tagp}.m{j}.{c}.{nh}")
            if lora:
                for nh in range(2):
                    em.op("pe", lambda e, nh=nh, j=j, pX=pX: e.matmul(
                        pX[:, nh * 512:(nh + 1) * 512], BIGB[:, 4096 + j * 128: 4096 + (j + 1) * 128],
                        SCR[:, 6144 + nh * 512: 6144 + (nh + 1) * 512], start=False, stop=True),
                        reads=RB(4096 + j * 128, 4096 + (j + 1) * 128) + ["babld", "S12", "S13"],
                        writes=[pXr[nh]], tag=f"{tagp}.l{j}.{nh}")
            sl = slab_buf(j % 2)
            slr = slab_res(j % 2)
            if lora:
                em.op("act", lambda e, sl=sl, pX=pX, j=j: e.activation(
                    sl, pX[:, :], AF.Identity, bias=biasl[:, j:j + 1], scale=1.0),
                    reads=pXr + ["biasl"], writes=slr, tag=f"{tagp}.e{j}")
            else:
                em.op("act", lambda e, sl=sl, pX=pX: e.copy(sl, pX[:, :]),
                      reads=pXr, writes=slr, tag=f"{tagp}.e{j}")
            spills.append(dma("sp", d_raw[:, :, j, :], sl.rearrange("p (t n) -> p t n", n=128),
                              reads=slr, writes=[], tag=f"{tagp}.s{j}"))
        return spills

    spillD = big_matmul_spill(d_wo, d_raw1, "D", npre=2)

    def apply_phase(d_raw, tagp, with_ln3, spill_ids, x_src_fn, out_sink_fn):
        def tile_bufs(to):
            rawfm = bigb_f32(4096 * (to % 2), 4096)
            rfr = RB(*xslot_cols(to))
            xt = bigb_f32(8192 + 4096 * (to % 2), 4096)
            xtr = RB(16384 + 8192 * (to % 2), 24576 + 8192 * (to % 2))
            return rawfm, rfr, xt, xtr

        def emit_dmas(to):
            rawfm, rfr, xt, xtr = tile_bufs(to)
            rb = dma("actq", rawfm.rearrange("p (j n) -> p j n", n=128),
                     d_raw[:, to, :, :], reads=rfr, writes=rfr, tag=f"{tagp}.rb{to}")
            em.ops[rb]["deps"].update(spill_ids)
            x_src_fn(to, xt, xtr)

        all_banks = ["pC", "pD", "pE", "pF", "pA0", "pA1", "pB0", "pB1"]
        deferred = []
        emit_dmas(0)
        for to in range(NT_OWN):
            if to + 1 < NT_OWN:
                emit_dmas(to + 1)
            rawfm, rfr, xt, xtr = tile_bufs(to)
            par = to % 2
            lo0 = 8192 * (to % 2)
            for jg in range(0, 32, 4):
                bank = all_banks[jg // 4]
                bp = PS[bank]
                for j in range(jg, jg + 4):
                    em.op("pe", lambda e, j=j, jg=jg, bp=bp, rawfm=rawfm: e.transpose(
                        bp[:, (j - jg) * 128:(j - jg + 1) * 128], rawfm[:, j * 128:(j + 1) * 128], idf[:, :]),
                        reads=rfr + ["idf"], writes=[bank], tag=f"{tagp}.tr{to}.{j}")
                emit_bn_chunk(bp, [bank], par, jg // 4, f"{tagp}.{to}")
            emit_ln_finish(par, f"{tagp}.{to}")
            em.op("act", lambda e, xt=xt, par=par: e.activation(
                xt, xt, AF.Identity, bias=nbias_[par][:, 0:1], scale=1.0),
                reads=xtr + [f"nbias{par}"], writes=xtr, tag=f"{tagp}.xb{to}")
            for jg in range(0, 32, 4):
                bank = all_banks[jg // 4]
                bp = PS[bank]
                trr = RB(lo0 + jg * 256, lo0 + (jg + 4) * 256)
                em.op("dve", lambda e, bp=bp, jg=jg, rawfm=rawfm, xt=xt, par=par: e.scalar_tensor_tensor(
                    rawfm[:, jg * 128:(jg + 4) * 128], bp[:, :], rstd_[par][:, 0:1],
                    xt[:, jg * 128:(jg + 4) * 128], ALU.mult, ALU.add),
                    reads=[bank, f"rstd{par}"] + xtr + trr, writes=trr, tag=f"{tagp}.st{to}.{jg}")
            if deferred:
                deferred.pop(0)()
            out_sink_fn(to, rawfm, rfr)
            if with_ln3:
                emit_ln_stats(rawfm, rfr, f"{tagp}.3.{to}", par=par)
                lt = SCR[:, 4096 * (to % 2):4096 * (to % 2) + 4096]
                ltr = RS(*ln1t_cols(to))
                em.op("act", lambda e, rawfm=rawfm, lt=lt, par=par: e.activation(
                    lt, rawfm, AF.Identity, bias=nbias_[par][:, 0:1], scale=rstd_[par][:, 0:1]),
                    reads=rfr + [f"rstd{par}", f"nbias{par}"], writes=ltr, tag=f"{tagp}.l3{to}")
                def l3tr(to=to, lt=lt, ltr=ltr):
                    for cg in range(0, 32, 8):
                        bank = all_banks[cg // 8]
                        bview = PS[bank].bitcast(BF16)
                        for c in range(cg, cg + 8):
                            em.op("pe", lambda e, c=c, cg=cg, bv=bview, lt=lt: e.transpose(
                                bv[:, (c - cg) * 128:(c - cg + 1) * 128], lt[:, c * 128:(c + 1) * 128], idb[:, :]),
                                reads=ltr + ["idb"], writes=[bank], tag=f"{tagp}.l3t{to}.{c}")
                        out_ap = BIGA.rearrange("p (c n) -> p c n", n=1024)[:, cg:cg + 8, to * 128:(to + 1) * 128]
                        leng = "dve" if (cg // 8) % 2 == 0 else "act"
                        lfn = (lambda e, bv=bview, oa=out_ap: e.tensor_copy(oa, bv.rearrange("p (c n) -> p c n", n=128))) \
                            if leng == "dve" else \
                            (lambda e, bv=bview, oa=out_ap: e.copy(oa, bv.rearrange("p (c n) -> p c n", n=128)))
                        em.op(leng, lfn,
                            reads=[bank], writes=RA(cg * 1024, (cg + 8) * 1024), tag=f"{tagp}.l3e{to}.{cg}")
                deferred.append(l3tr)
        while deferred:
            deferred.pop(0)()

    def flush_deferred():
        pass

    def x_from_input(to, xt, xtr):
        dma("actq", xt, d_x[QOFF + to * 128: QOFF + (to + 1) * 128, :],
            reads=xtr, writes=xtr, tag=f"E.x{to}")

    x2spills = {}

    def x2_sink(to, x2ap, x2res):
        x2spills[to] = dma("sp", d_x2[to], x2ap, reads=x2res, writes=[], tag=f"E.x2s{to}")

    apply_phase(d_raw1, "E", True, spillD, x_from_input, x2_sink)

    aab = BIGB[:, 0:4096]
    bab = BIGB[:, 4096:8192]
    hw_ap = SCR[:, 6144:7168]
    c128 = scr_f32(2048, 1024)

    dma("pool", aab, d_aa[:, :], reads=RB(0, 4096), writes=RB(0, 4096) + ["aabld"], tag="F.aa")
    dma("pool", bab, d_ba[:, :], reads=RB(4096, 8192), writes=RB(4096, 8192) + ["babld"], tag="F.ba")
    w_prefetch(d_wb, 0, "G")
    w_prefetch(d_wb, 1, "G")

    for c in range(NCH):
        for nh, (bk, bp) in enumerate([("pE", psE), ("pF", psF)]):
            em.op("pe", lambda e, c=c, nh=nh, bp=bp: e.matmul(
                bp[0:8, 0:512], wrb[:, c * 8:(c + 1) * 8],
                BIGA[:, c * 1024 + nh * 512: c * 1024 + (nh + 1) * 512],
                start=(c == 0), stop=(c == 31)),
                reads=["wrb"] + RA(c * 1024 + nh * 512, c * 1024 + (nh + 1) * 512),
                writes=[bk], tag=f"F.r{c}.{nh}")
    em.op("dve", lambda e: e.tensor_copy(lgT[:, 0:512], psE[0:8, 0:512]),
          reads=["pE"], writes=["lgT.0", "S0", "S1"], tag="F.le0")
    em.op("dve", lambda e: e.tensor_copy(lgT[:, 512:1024], psF[0:8, 0:512]),
          reads=["pF"], writes=["lgT.1", "S2", "S3"], tag="F.le1")
    for g in range(2):
        bank = tr_banks[g]
        bp = PS[bank]
        for c8 in range(g * 4, g * 4 + 4):
            em.op("pe", lambda e, c8=c8, g=g, bp=bp: e.transpose(
                bp[:, (c8 - g * 4) * 8:(c8 - g * 4 + 1) * 8], lgT[:, c8 * 128:(c8 + 1) * 128], idf[0:8, 0:8]),
                reads=[f"lgT.{c8 // 4}", "S0", "S1", "S2", "S3", "idf"], writes=[bank], tag=f"F.lt{c8}")
        em.op("dve", lambda e, bp=bp, g=g: e.tensor_copy(lgtm[:, g * 32:(g + 1) * 32], bp[:, 0:32]),
              reads=[bank], writes=[f"lgtm.{g}"], tag=f"F.lte{g}")
    for c8 in range(8):
        lg8 = lgtm[:, c8 * 8:(c8 + 1) * 8]
        em.op("dve", lambda e, lg8=lg8: e.reduce_max(mx1[:, :], lg8, AX.X),
              reads=[f"lgtm.{c8 // 4}"], writes=["mx1"], tag=f"F.m{c8}")
        em.op("dve", lambda e: e.tensor_scalar(mx1[:, :], mx1[:, :], -1.0, None, ALU.mult),
              reads=["mx1"], writes=["mx1"], tag=f"F.n{c8}")
        em.op("act", lambda e, lg8=lg8: e.activation(ee8[:, :], lg8, AF.Exp, bias=mx1[:, 0:1], scale=1.0),
              reads=[f"lgtm.{c8 // 4}", "mx1"], writes=["ee8"], tag=f"F.e{c8}")
        em.op("dve", lambda e: e.max(out=mx8[:, :], in_=ee8[:, :]),
              reads=["ee8"], writes=["mx8"], tag=f"F.x8{c8}")
        em.op("dve", lambda e: e.memset(scr8[:, :], 0.0), reads=[], writes=["scr8"], tag=f"F.z{c8}")
        em.op("dve", lambda e: e.tensor_copy(scr8[:, 0:2], mx8[:, 0:2]),
              reads=["mx8"], writes=["scr8"], tag=f"F.c2{c8}")
        em.op("dve", lambda e: e.match_replace(out=mr8[:, :], in_to_replace=scr8[:, :],
                                               in_values=ee8[:, :], imm_value=0.0),
              reads=["scr8", "ee8"], writes=["mr8"], tag=f"F.mr{c8}")
        em.op("dve", lambda e: e.tensor_sub(mr8[:, :], ee8[:, :], mr8[:, :]),
              reads=["ee8", "mr8"], writes=["mr8"], tag=f"F.k{c8}")
        em.op("dve", lambda e: e.tensor_tensor(den[:, :], mx8[:, 0:1], mx8[:, 1:2], ALU.add),
              reads=["mx8"], writes=["den"], tag=f"F.d{c8}")
        em.op("dve", lambda e: e.reciprocal(rden1[:, :], den[:, :]),
              reads=["den"], writes=["rden1"], tag=f"F.rd{c8}")
        em.op("dve", lambda e, c8=c8: e.tensor_scalar(cmbtm[:, c8 * 8:(c8 + 1) * 8], mr8[:, :],
                                                      rden1[:, 0:1], 2.0, ALU.mult, ALU.mult),
              reads=["mr8", "rden1"], writes=[f"cmbtm.{c8}"], tag=f"F.cm{c8}")
    for g in range(2):
        bank = tr_banks[2 + g]
        bp = PS[bank]
        for c8 in range(g * 4, g * 4 + 4):
            em.op("pe", lambda e, c8=c8, bp=bp: e.transpose(
                bp[0:8, (c8 % 4) * 128:(c8 % 4 + 1) * 128], cmbtm[:, c8 * 8:(c8 + 1) * 8], idf[:, :]),
                reads=[f"cmbtm.{c8}", "idf"], writes=[bank], tag=f"F.ct{c8}")
        em.op("dve", lambda e, bp=bp, g=g: e.tensor_copy(cmbT[:, g * 512:(g + 1) * 512], bp[0:8, 0:512]),
              reads=[bank, f"lgT.{g}"], writes=[f"cmbT.{g}", f"lgT.{g}", f"S{2*g}", f"S{2*g+1}"], tag=f"F.cte{g}")
    for nh, (bk, bp) in enumerate([("pE", psE), ("pF", psF)]):
        em.op("pe", lambda e, nh=nh, bp=bp: e.matmul(
            bp[:, 0:512], e8f[:, :], cmbT[0:8, nh * 512:(nh + 1) * 512], start=True, stop=True),
            reads=[f"cmbT.{nh}", f"S{2*nh}", f"S{2*nh+1}", "e8f"], writes=[bk], tag=f"F.ex{nh}")
        em.op("dve", lambda e, nh=nh, bp=bp, c128=c128: e.tensor_copy(
            c128[:, nh * 512:(nh + 1) * 512], bp[:, 0:512]),
            reads=[bk], writes=[f"c128.{nh}", f"S{8 + 2 * nh}", f"S{9 + 2 * nh}"], tag=f"F.exe{nh}")
    for c in range(NCH):
        for nh in range(2):
            em.op("pe", lambda e, c=c, nh=nh: e.matmul(
                psB[:, nh * 512:(nh + 1) * 512], aab[:, c * 128:(c + 1) * 128],
                BIGA[:, c * 1024 + nh * 512: c * 1024 + (nh + 1) * 512],
                start=(c == 0), stop=(c == 31)),
                reads=["aabld"] + RB(c * 128, (c + 1) * 128) + RA(c * 1024 + nh * 512, c * 1024 + (nh + 1) * 512),
                writes=[f"pB{nh}"], tag=f"F.h{c}.{nh}")
    for nh in range(2):
        em.op("dve", lambda e, nh=nh, c128=c128: e.tensor_tensor(
            hw_ap[:, nh * 512:(nh + 1) * 512], psB[:, nh * 512:(nh + 1) * 512],
            c128[:, nh * 512:(nh + 1) * 512], ALU.mult),
            reads=[f"pB{nh}", f"c128.{nh}"], writes=["S12", "S13"], tag=f"F.hw{nh}")

    spillG = big_matmul_spill(d_wb, d_raw2, "G", lora=True, npre=2)

    st_ids = []

    def x2_from_scratch(to, xt, xtr):
        rb = dma("sp", xt, d_x2[to], reads=xtr, writes=xtr, tag=f"H.x2{to}")
        em.ops[rb]["deps"].add(x2spills[to])

    def out_sink(to, outap, outres):
        st_ids.append(dma("st", d_out[to * 128:(to + 1) * 128, :], outap,
                          reads=outres, writes=[], tag=f"H.out{to}"))

    apply_phase(d_raw2, "H", False, spillG, x2_from_scratch, out_sink)

    fin = em.op("sp", lambda e: e.nop(), reads=[], writes=[], tag="fin")
    em.ops[fin]["deps"].update(st_ids)

    streams = em.build(sems, dsems)

    with stack, nc.Block() as block:
        @block.tensor
        def _(e):
            for f in streams["pe"]:
                f(e)

        @block.scalar
        def _(e):
            for f in streams["act"]:
                f(e)

        @block.vector
        def _(e):
            for f in streams["dve"]:
                f(e)

        @block.gpsimd
        def _(e):
            for f in streams["pool"]:
                f(e)

        @block.sync
        def _(e):
            for f in streams["sp"]:
                f(e)

    return nc



def _tile_weight(w):
    t = w.reshape(32, 128, 32, 128)
    t = np.ascontiguousarray(t.transpose(0, 3, 2, 1))
    return t.reshape(32, 128, 32 * 128)


def _tile_small(w):
    O = w.shape[0]
    t = w.reshape(O, 32, 128)
    t = np.ascontiguousarray(t.transpose(2, 1, 0))
    return t.reshape(128, 32 * O)


def _rope_tables(lo, n):
    inv = 1.0 / (10000.0 ** (np.arange(0, 128, 2, dtype=np.float32) / 128.0))
    t = np.arange(lo, lo + n, dtype=np.float32)
    f = np.outer(t, inv)
    cos = np.zeros((n, 128), dtype=np.float32)
    sin = np.zeros((n, 128), dtype=np.float32)
    cos[:, 0::2] = np.cos(f)
    cos[:, 1::2] = np.cos(f)
    sin[:, 0::2] = np.sin(f)
    sin[:, 1::2] = np.sin(f)
    sin_eff = sin.copy()
    sin_eff[:, 0:64] = -sin_eff[:, 0:64]
    return np.ascontiguousarray(cos.T), np.ascontiguousarray(sin_eff.T)


def kernel(L0_attn_x, L0_attn_Wq, L0_attn_Wk, L0_attn_Wv, L0_attn_Wo,
           L0_ffn_W_base, L0_ffn_b_base, L0_ffn_W_router, L0_ffn_A, L0_ffn_B):
    x = np.asarray(L0_attn_x, dtype=np.float32)
    wq_l = _tile_weight(np.asarray(L0_attn_Wq, dtype=np.float32))
    wo_l = _tile_weight(np.asarray(L0_attn_Wo, dtype=np.float32))
    wb_l = _tile_weight(np.asarray(L0_ffn_W_base, dtype=np.float32))
    wk_l = _tile_small(np.asarray(L0_attn_Wk, dtype=np.float32))
    wv_l = _tile_small(np.asarray(L0_attn_Wv, dtype=np.float32))
    wr_l = _tile_small(np.asarray(L0_ffn_W_router, dtype=np.float32))
    A = np.asarray(L0_ffn_A, dtype=np.float32)
    Bm = np.asarray(L0_ffn_B, dtype=np.float32)
    aa_l = np.ascontiguousarray(A.reshape(128, 32, 128).transpose(2, 1, 0)).reshape(128, 4096)
    ba_l = np.ascontiguousarray(Bm.transpose(0, 2, 1)).reshape(128, 4096)
    bb_l = np.ascontiguousarray(np.asarray(L0_ffn_b_base, dtype=np.float32).reshape(32, 128).T)

    alpha = np.linspace(0.0, 1.0, 256, dtype=np.float32)
    idb = np.eye(128, dtype=NPBF16)
    idf = np.eye(128, dtype=np.float32)
    onesc = np.ones((128, 128), dtype=NPBF16)
    o1c = np.ones((1, 128), dtype=NPBF16)
    e8 = np.zeros((8, 128), dtype=np.float32)
    for ei in range(8):
        e8[ei, ei * 16:(ei + 1) * 16] = 1.0

    in_maps = []
    for c in range(8):
        b, half = c // 2, c % 2
        qstart = half * 1024
        lo = qstart - 256
        xs = np.zeros((NSH, D), dtype=np.float32)
        glo, ghi = max(0, lo), min(S, lo + NSH)
        xs[glo - lo: ghi - lo] = x[b, glo:ghi]
        cosT, sinT = _rope_tables(lo, NSH)
        wv = np.zeros((128, 1536), dtype=np.float32)
        wv[:, 0:256] = 0.0 if half == 0 else (1.0 - alpha)
        wv[:, 256:512] = 1.0 if half == 0 else alpha
        wv[:, 512:768] = 1.0 - alpha
        wv[:, 768:1024] = alpha
        wv[:, 1024:1280] = 1.0 - alpha
        wv[:, 1280:1536] = alpha
        padc = np.zeros((1, 1024), dtype=NPBF16)
        if half == 1:
            padc[0, 512:768] = NPBF16(-128.0)
        in_maps.append({
            "x_sh": xs, "wq_l": wq_l, "wo_l": wo_l, "wb_l": wb_l,
            "wk_l": wk_l, "wv_l": wv_l, "wr_l": wr_l,
            "aa_l": aa_l, "ba_l": ba_l, "bb_l": bb_l,
            "cos_l": cosT, "sin_l": sinT, "wv_c": wv, "pad_c": padc,
            "idb_c": idb, "idf_c": idf, "ones_c": onesc, "o1_c": o1c, "e8_c": e8,
        })

    nc = build_program()
    res = run_bass_kernel_spmd(nc, in_maps, core_ids=list(range(8)))
    out = np.empty((B_, S, D), dtype=np.float32)
    for c in range(8):
        b, half = c // 2, c % 2
        out[b, half * 1024:(half + 1) * 1024] = res.results[c]["out_sh"]
    return out


# version 38 (final)
# speedup vs baseline: 1.0114x; 1.0010x over previous
import numpy as np
import ml_dtypes
from contextlib import ExitStack

import concourse.bass as bass
import concourse.mybir as mybir
from concourse.bass_utils import run_bass_kernel_spmd

F32 = mybir.dt.float32
BF16 = mybir.dt.bfloat16
AF = mybir.ActivationFunctionType
ALU = mybir.AluOpType
AX = mybir.AxisListType
NPBF16 = ml_dtypes.bfloat16

D = 4096
NCH = 32
B_ = 4
S = 2048
NTOK = 1024
NSH = 1536
NT_SH = 12
NT_OWN = 8
QOFF = 256
H = 32
EPS = 1e-5
SCALE = float(1.0 / np.sqrt(128.0))

SAME_ENGINE_SYNC = True

ENG_COMPUTE = ("pe", "act", "dve", "pool", "sp")


class Em:

    def __init__(self):
        self.ops = []
        self.res = {}

    def op(self, engine, fn, reads=(), writes=(), dma=None, tag=""):
        opid = len(self.ops)
        deps = set()
        for r in reads:
            st = self.res.get(r)
            if st is not None and st["w"] is not None:
                deps.add(st["w"])
        for w in writes:
            st = self.res.get(w)
            if st is not None:
                if st["w"] is not None:
                    deps.add(st["w"])
                deps.update(st["r"])
        for r in reads:
            self.res.setdefault(r, {"w": None, "r": []})["r"].append(opid)
        for w in writes:
            self.res[w] = {"w": opid, "r": []}
        deps.discard(opid)
        self.ops.append(dict(engine=engine, fn=fn, deps=deps, dma=dma, tag=tag))
        return opid

    def build(self, sems, dsems):
        needed = set()
        for op in self.ops:
            needed |= op["deps"]
        cnt = {e: 0 for e in ENG_COMPUTE}
        dcnt = {q: 0 for q in dsems}
        meta = []
        for i, op in enumerate(self.ops):
            if op["dma"] is not None:
                dcnt[op["dma"]] += 1
                meta.append(("dma", op["dma"], dcnt[op["dma"]]))
            elif i in needed:
                cnt[op["engine"]] += 1
                meta.append(("mark", op["engine"], cnt[op["engine"]]))
            else:
                meta.append(("plain", op["engine"], cnt[op["engine"]]))
        streams = {e: [] for e in ENG_COMPUTE}
        watermark = {e: {} for e in ENG_COMPUTE}
        for i, op in enumerate(self.ops):
            eng = op["engine"]
            waits = {}
            for d in op["deps"]:
                kind, key, val = meta[d]
                if kind == "dma":
                    sk = ("dma", key)
                    waits[sk] = max(waits.get(sk, 0), 16 * val)
                else:
                    deng = self.ops[d]["engine"]
                    if deng == eng and (not SAME_ENGINE_SYNC or deng == "pe"):
                        continue
                    if kind == "plain":
                        raise RuntimeError(f"dep on unmarked op {d} ({self.ops[d]['tag']}) from {i} ({op['tag']})")
                    sk = ("eng", key)
                    waits[sk] = max(waits.get(sk, 0), val)
            wm = watermark[eng]
            emit_waits = []
            for sk, val in sorted(waits.items()):
                if wm.get(sk, 0) < val:
                    wm[sk] = val
                    emit_waits.append((sk, val))
            kind, key, val = meta[i]

            def make(emit_waits=emit_waits, op=op, kind=kind, key=key):
                def run(eh):
                    for (skind, sname), v in emit_waits:
                        sem = dsems[sname] if skind == "dma" else sems[sname]
                        eh.wait_ge(sem, v)
                    inst = op["fn"](eh)
                    if kind == "dma":
                        inst.then_inc(dsems[key], 16)
                    elif kind == "mark":
                        inst.then_inc(sems[key], 1)
                return run
            streams[eng].append(make())
        return streams


def build_program():
    nc = bass.Bass("TRN2", target_bir_lowering=False)

    d_x = nc.dram_tensor("x_sh", [NSH, D], F32, kind="ExternalInput")
    d_wq = nc.dram_tensor("wq_l", [NCH, 128, NCH * 128], F32, kind="ExternalInput")
    d_wo = nc.dram_tensor("wo_l", [NCH, 128, NCH * 128], F32, kind="ExternalInput")
    d_wb = nc.dram_tensor("wb_l", [NCH, 128, NCH * 128], F32, kind="ExternalInput")
    d_wk = nc.dram_tensor("wk_l", [128, NCH * 128], F32, kind="ExternalInput")
    d_wv = nc.dram_tensor("wv_l", [128, NCH * 128], F32, kind="ExternalInput")
    d_wr = nc.dram_tensor("wr_l", [128, NCH * 8], F32, kind="ExternalInput")
    d_aa = nc.dram_tensor("aa_l", [128, NCH * 128], F32, kind="ExternalInput")
    d_ba = nc.dram_tensor("ba_l", [128, NCH * 128], F32, kind="ExternalInput")
    d_bb = nc.dram_tensor("bb_l", [128, NCH], F32, kind="ExternalInput")
    d_cos = nc.dram_tensor("cos_l", [128, NSH], F32, kind="ExternalInput")
    d_sin = nc.dram_tensor("sin_l", [128, NSH], F32, kind="ExternalInput")
    d_wvv = nc.dram_tensor("wv_c", [128, 1536], F32, kind="ExternalInput")
    d_pad = nc.dram_tensor("pad_c", [1, 1024], BF16, kind="ExternalInput")
    d_idb = nc.dram_tensor("idb_c", [128, 128], BF16, kind="ExternalInput")
    d_idf = nc.dram_tensor("idf_c", [128, 128], F32, kind="ExternalInput")
    d_ones = nc.dram_tensor("ones_c", [128, 128], BF16, kind="ExternalInput")
    d_o1 = nc.dram_tensor("o1_c", [1, 128], BF16, kind="ExternalInput")
    d_e8 = nc.dram_tensor("e8_c", [8, 128], F32, kind="ExternalInput")

    d_out = nc.dram_tensor("out_sh", [NTOK, D], F32, kind="ExternalOutput")

    d_raw1 = nc.dram_tensor("raw1_d", [128, NT_OWN, NCH, 128], F32)
    d_raw2 = nc.dram_tensor("raw2_d", [128, NT_OWN, NCH, 128], F32)
    d_x2 = nc.dram_tensor("x2_d", [NT_OWN, 128, D], F32)

    stack = ExitStack()
    sb = lambda nm, sh, dt=F32: stack.enter_context(nc.sbuf_tensor(nm, sh, dt))

    cosT = sb("cosT", [128, NSH])
    sinT = sb("sinT", [128, NSH])
    idb = sb("idb", [128, 128], BF16)
    idf = sb("idf", [128, 128])
    onesb = sb("onesb", [128, 128], BF16)
    o1b = sb("o1b", [1, 128], BF16)
    e8f = sb("e8f", [8, 128])
    wvc = sb("wvc", [128, 1536])
    padr = sb("padr", [1, 1024], BF16)
    wrb = sb("wrb", [128, NCH * 8], BF16)
    biasl = sb("biasl", [128, NCH])
    wbuf = [sb("wbuf0", [128, NCH * 128], BF16), sb("wbuf1", [128, NCH * 128], BF16)]
    BIGA = sb("BIGA", [128, 32 * 1024], BF16)
    BIGB = sb("BIGB", [128, 32 * 1024], BF16)
    SCR = sb("SCR", [128, 8 * 1024], BF16)
    kr = sb("kr", [128, NSH], BF16)
    vt = sb("vt", [128, NSH], BF16)
    bn6_ = [sb("bn6a", [128, 48]), sb("bn6b", [128, 48])]
    mv_ = [sb("mva", [128, 2]), sb("mvb", [128, 2])]
    sqv_ = [sb("sqva", [128, 1]), sb("sqvb", [128, 1])]
    rstd_ = [sb("rstda", [128, 1]), sb("rstdb", [128, 1])]
    nbias_ = [sb("nbiasa", [128, 1]), sb("nbiasb", [128, 1])]
    tmpA = sb("tmpA", [128, 256])
    csb = [sb("csb0", [128, 512]), sb("csb1", [128, 512]), sb("csb2", [128, 512])]
    tails = [sb(f"tail{k}", [128, 256]) for k in range(4)]
    lgT = SCR[0:8, 0:2048].bitcast(F32)
    cmbT = lgT
    lgtm = sb("lgtm", [128, 64])
    cmbtm = sb("cmbtm", [128, 64])
    mx1 = sb("mx1", [128, 1])
    mx8 = sb("mx8", [128, 8])
    scr8 = sb("scr8", [128, 8])
    mr8 = sb("mr8", [128, 8])
    ee8 = sb("ee8", [128, 8])
    den = sb("den", [128, 1])
    rden1 = sb("rden1", [128, 1])
    epsv = sb("epsv", [128, 1])

    psA = stack.enter_context(nc.psum_tensor("psA", [128, 1024], F32))
    psB = stack.enter_context(nc.psum_tensor("psB", [128, 1024], F32))
    psC = stack.enter_context(nc.psum_tensor("psC", [128, 512], F32))
    psD = stack.enter_context(nc.psum_tensor("psD", [128, 512], F32))
    psE = stack.enter_context(nc.psum_tensor("psE", [128, 512], F32))
    psF = stack.enter_context(nc.psum_tensor("psF", [128, 512], F32))

    sems = {e: stack.enter_context(nc.semaphore(f"s_{e}")) for e in ENG_COMPUTE}
    dsems = {"sp": stack.enter_context(nc.semaphore("d_sp")),
             "pool": stack.enter_context(nc.semaphore("d_pool")),
             "actq": stack.enter_context(nc.semaphore("d_actq")),
             "st": stack.enter_context(nc.semaphore("d_st"))}

    em = Em()

    def _regs(pref, lo, hi, g):
        return [f"{pref}{k}" for k in range(lo // g, (hi + g - 1) // g)]

    def RA(lo, hi):
        return _regs("A", lo, hi, 512)

    def RB(lo, hi):
        return _regs("B", lo, hi, 512)

    def RS(lo, hi):
        return _regs("S", lo, hi, 512)

    def bigb_f32(col_f32, n_f32):
        return BIGB[:, 2 * col_f32: 2 * (col_f32 + n_f32)].bitcast(F32)

    def scr_f32(col_f32, n_f32):
        return SCR[:, 2 * col_f32: 2 * (col_f32 + n_f32)].bitcast(F32)

    PS = {"pA0": psA[:, 0:512], "pA1": psA[:, 512:1024],
          "pB0": psB[:, 0:512], "pB1": psB[:, 512:1024],
          "pC": psC[:, :], "pD": psD[:, :], "pE": psE[:, :], "pF": psF[:, :]}

    def dma(queue, dst, src, reads=(), writes=(), tag=""):
        host = {"sp": "sp", "pool": "pool", "actq": "act", "st": "sp"}[queue]
        return em.op(host, lambda e, d=dst, s=src: e.dma_start(d, s),
                     reads=reads, writes=writes, dma=queue, tag=tag)

    def w_prefetch(d_w, j, tagp):
        wres = [f"wb{j % 2}"]
        dma("pool", wbuf[j % 2][:, :], d_w[j, :, :], reads=wres, writes=wres, tag=f"{tagp}.w{j}")

    dma("sp", cosT[:, :], d_cos[:, :], writes=["cosT"])
    dma("sp", sinT[:, :], d_sin[:, :], writes=["sinT"])
    dma("sp", idb[:, :], d_idb[:, :], writes=["idb"])
    dma("sp", idf[:, :], d_idf[:, :], writes=["idf"])
    dma("sp", onesb[:, :], d_ones[:, :], writes=["onesb"])
    dma("sp", o1b[:, :], d_o1[:, :], writes=["o1b"])
    dma("sp", e8f[:, :], d_e8[:, :], writes=["e8f"])
    dma("sp", wvc[:, :], d_wvv[:, :], writes=["wvc"])
    dma("sp", padr[:, :], d_pad[:, :], writes=["padr"])
    dma("sp", biasl[:, :], d_bb[:, :], writes=["biasl"])
    em.op("dve", lambda e: e.memset(epsv[:, :], float(EPS)), reads=[], writes=["epsv"], tag="init.eps")
    dma("pool", wrb[:, :], d_wr[:, :], writes=["wrb"])
    dma("pool", wbuf[0][:, :], d_wk[:, :], writes=["wb0"])
    dma("pool", wbuf[1][:, :], d_wv[:, :], writes=["wb1"])

    def emit_bn_chunk(src_ap, src_res, par, cc, tagp):
        bn6 = bn6_[par]
        sl = src_ap if src_ap.shape[-1] == 512 else src_ap[:, cc * 512:(cc + 1) * 512]
        em.op("dve", lambda e, cc=cc, sl=sl, bn6=bn6: e.bn_stats(
            bn6[:, cc * 6:(cc + 1) * 6], sl),
            reads=src_res, writes=[f"bn6{par}.{cc}"], tag=f"{tagp}.bn{cc}")

    def emit_ln_finish(par, tagp):
        bn6, mv, sqv, rstd, nbias = bn6_[par], mv_[par], sqv_[par], rstd_[par], nbias_[par]
        em.op("dve", lambda e, bn6=bn6, mv=mv: e.bn_aggr(mv[:, :], bn6.rearrange("p (c s) -> p c s", s=6)),
              reads=[f"bn6{par}.{cc}" for cc in range(8)], writes=[f"mv{par}"], tag=f"{tagp}.bnagg")
        em.op("act", lambda e, mv=mv, sqv=sqv: e.activation(sqv[:, :], mv[:, 1:2], AF.Sqrt, bias=epsv[:, 0:1], scale=1.0),
              reads=[f"mv{par}", "epsv"], writes=[f"sqv{par}"], tag=f"{tagp}.sqrt")
        em.op("dve", lambda e, sqv=sqv, rstd=rstd: e.reciprocal(rstd[:, :], sqv[:, :]),
              reads=[f"sqv{par}"], writes=[f"rstd{par}"], tag=f"{tagp}.rcp")
        em.op("dve", lambda e, mv=mv, rstd=rstd, nbias=nbias: e.tensor_scalar(
            nbias[:, :], mv[:, 0:1], rstd[:, 0:1], -1.0, ALU.mult, ALU.mult),
            reads=[f"mv{par}", f"rstd{par}"], writes=[f"nbias{par}"], tag=f"{tagp}.nb")

    def emit_ln_stats(src_ap, src_res, tagp, par=0):
        for cc in range(8):
            emit_bn_chunk(src_ap, src_res, par, cc, tagp)
        emit_ln_finish(par, tagp)

    tr_banks = ["pC", "pD", "pE", "pF"]

    def xslot(i):
        return bigb_f32(4096 * (i % 2), 4096)

    def xslot_cols(i):
        return (8192 * (i % 2), 8192 * (i % 2) + 8192)

    hxF = BIGB[:, 16384:24576]
    hxB = BIGB[:, 24576:32768]

    OWN = list(range(2, 10))

    def ln1t_buf(i):
        return SCR[:, 4096 * (i % 2): 4096 * (i % 2) + 4096]

    def ln1t_cols(i):
        return (4096 * (i % 2), 4096 * (i % 2) + 4096)

    def s1_dma(t):
        xr = RB(*xslot_cols(t))
        dma("sp", xslot(t), d_x[t * 128:(t + 1) * 128, :], reads=xr, writes=xr, tag=f"A.x{t}")

    def s1_stats(t):
        xr = RB(*xslot_cols(t))
        emit_ln_stats(xslot(t), xr, f"A.{t}", par=t % 2)
        lr = RS(*ln1t_cols(t))
        em.op("act", lambda e, t=t: e.activation(ln1t_buf(t), xslot(t), AF.Identity,
                                                 bias=nbias_[t % 2][:, 0:1], scale=rstd_[t % 2][:, 0:1]),
              reads=xr + [f"rstd{t % 2}", f"nbias{t % 2}"], writes=lr, tag=f"A.ap{t}")

    def s1_tr(t):
        lr = RS(*ln1t_cols(t))
        for cg in range(0, 32, 8):
            bank = tr_banks[(cg // 8) % 4]
            bview = PS[bank].bitcast(BF16)
            for c in range(cg, cg + 8):
                em.op("pe", lambda e, c=c, cg=cg, t=t, bv=bview: e.transpose(
                    bv[:, (c - cg) * 128:(c - cg + 1) * 128],
                    ln1t_buf(t)[:, c * 128:(c + 1) * 128], idb[:, :]),
                    reads=lr + ["idb"], writes=[bank], tag=f"A.tr{t}.{c}")
            if t in OWN:
                to = t - 2
                out_ap = BIGA.rearrange("p (c n) -> p c n", n=1024)[:, cg:cg + 8, to * 128:(to + 1) * 128]
                wres = RA(cg * 1024, (cg + 8) * 1024)
            else:
                hx = hxF if t < 2 else hxB
                base = 16384 if t < 2 else 24576
                ti = t % 2
                out_ap = hx.rearrange("p (c n) -> p c n", n=256)[:, cg:cg + 8, ti * 128:(ti + 1) * 128]
                wres = RB(base + cg * 256, base + (cg + 8) * 256)
            em.op("dve", lambda e, bv=bview, oa=out_ap: e.tensor_copy(
                oa, bv.rearrange("p (c n) -> p c n", n=128)),
                reads=[bank], writes=wres, tag=f"A.trev{t}.{cg}")

    s1_dma(OWN[0])
    tpend = None
    for j, t in enumerate(OWN):
        if j + 1 < len(OWN):
            s1_dma(OWN[j + 1])
        s1_stats(t)
        if tpend is not None:
            s1_tr(tpend)
        tpend = t
    s1_tr(tpend)

    KVG = [
        (lambda c: hxF[:, c * 256:(c + 1) * 256], lambda c: RB(16384 + c * 256, 16384 + (c + 1) * 256), 0, 256, [0, 1]),
        (lambda c: BIGA[:, c * 1024 + 0: c * 1024 + 512], lambda c: RA(c * 1024, c * 1024 + 512), 256, 512, [2, 3, 4, 5]),
        (lambda c: BIGA[:, c * 1024 + 512: c * 1024 + 1024], lambda c: RA(c * 1024 + 512, c * 1024 + 1024), 768, 512, [6, 7, 8, 9]),
        (lambda c: hxB[:, c * 256:(c + 1) * 256], lambda c: RB(24576 + c * 256, 24576 + (c + 1) * 256), 1280, 256, [10, 11]),
    ]
    vsb5 = csb[2][:, :].bitcast(BF16)[:, 0:512]

    def kv_group(gi):
        (rhs_fn, rres_fn, klo, n, ktiles) = KVG[gi]
        krres = [f"kr{t}" for t in ktiles]
        vtres = [f"vt{t}" for t in ktiles]
        for c in range(NCH):
            em.op("pe", lambda e, c=c, rhs_fn=rhs_fn, n=n: e.matmul(
                psC[:, 0:n], wbuf[0][:, c * 128:(c + 1) * 128], rhs_fn(c),
                start=(c == 0), stop=(c == 31)),
                reads=["wb0"] + rres_fn(c), writes=["pC"], tag=f"A.k{gi}.{c}")
            em.op("pe", lambda e, c=c, rhs_fn=rhs_fn, n=n: e.matmul(
                psD[:, 0:n], wbuf[1][:, c * 128:(c + 1) * 128], rhs_fn(c),
                start=(c == 0), stop=(c == 31)),
                reads=["wb1"] + rres_fn(c), writes=["pD"], tag=f"A.v{gi}.{c}")
        qc = csb[0][:, 0:n]
        m2 = csb[1][:, 0:n]
        cs = cosT[:, klo:klo + n]
        sn = sinT[:, klo:klo + n]
        em.op("dve", lambda e, cs=cs, qc=qc, n=n: e.tensor_tensor(qc, psC[:, 0:n], cs, ALU.mult),
              reads=["pC", "cosT"], writes=["csb0"], tag=f"A.rk1.{gi}")
        em.op("dve", lambda e, sn=sn, m2=m2, n=n: e.tensor_tensor(m2[0:64, :], psC[64:128, 0:n], sn[0:64, :], ALU.mult),
              reads=["pC", "sinT"], writes=["csb1"], tag=f"A.rk2.{gi}")
        em.op("dve", lambda e, sn=sn, m2=m2, n=n: e.tensor_tensor(m2[64:128, :], psC[0:64, 0:n], sn[64:128, :], ALU.mult),
              reads=["pC", "sinT"], writes=["csb1b"], tag=f"A.rk3.{gi}")
        em.op("dve", lambda e, klo=klo, n=n, qc=qc, m2=m2: e.tensor_tensor(kr[:, klo:klo + n], qc, m2, ALU.add),
              reads=["csb0", "csb1", "csb1b"], writes=krres, tag=f"A.rk4.{gi}")
        em.op("act", lambda e, n=n: e.copy(vsb5[:, 0:n], psD[:, 0:n]),
              reads=["pD"], writes=["csb2"], tag=f"A.vc{gi}")
        for k in range(n // 128):
            bank = ["pE", "pF"][k % 2]
            bview = PS[bank].bitcast(BF16)
            em.op("pe", lambda e, k=k, bv=bview: e.transpose(
                bv[:, 0:128], vsb5[:, k * 128:(k + 1) * 128], idb[:, :]),
                reads=["csb2", "idb"], writes=[bank], tag=f"A.vt{gi}.{k}")
            em.op("dve", lambda e, k=k, klo=klo, bv=bview: e.tensor_copy(
                vt[:, klo + k * 128: klo + (k + 1) * 128], bv[:, 0:128]),
                reads=[bank], writes=vtres, tag=f"A.vte{gi}.{k}")

    s1_dma(0)
    s1_dma(1)
    kv_group(1)
    s1_stats(0)
    s1_stats(1)
    kv_group(2)
    s1_tr(0)
    s1_tr(1)
    s1_dma(10)
    s1_dma(11)
    s1_stats(10)
    s1_stats(11)
    s1_tr(10)
    s1_tr(11)
    kv_group(0)
    kv_group(3)

    def qr_head(h):
        return BIGB[:, h * 1024:(h + 1) * 1024]

    for h in range(H):
        wres = [f"wb{h % 2}"]
        dma("pool", wbuf[h % 2][:, :], d_wq[h, :, :], reads=wres, writes=wres, tag=f"B.w{h}")
        pX = psA if h % 2 == 0 else psB
        pXr = ["pA0", "pA1"] if h % 2 == 0 else ["pB0", "pB1"]
        for c in range(NCH):
            for nh in range(2):
                em.op("pe", lambda e, c=c, nh=nh, h=h, pX=pX: e.matmul(
                    pX[:, nh * 512:(nh + 1) * 512], wbuf[h % 2][:, c * 128:(c + 1) * 128],
                    BIGA[:, c * 1024 + nh * 512: c * 1024 + (nh + 1) * 512],
                    start=(c == 0), stop=(c == 31)),
                    reads=wres + RA(c * 1024 + nh * 512, c * 1024 + (nh + 1) * 512),
                    writes=[pXr[nh]], tag=f"B.q{h}.{c}.{nh}")
        for nh in range(2):
            qc = csb[0][:, :]
            m2 = csb[1][:, :]
            cs = cosT[:, QOFF + nh * 512: QOFF + (nh + 1) * 512]
            sn = sinT[:, QOFF + nh * 512: QOFF + (nh + 1) * 512]
            pslice = pX[:, nh * 512:(nh + 1) * 512]
            qres = RB(h * 1024 + nh * 512, h * 1024 + (nh + 1) * 512)
            em.op("dve", lambda e, pslice=pslice, cs=cs, qc=qc: e.tensor_tensor(qc, pslice, cs, ALU.mult),
                  reads=[pXr[nh], "cosT"], writes=["csb0"], tag=f"B.r1.{h}.{nh}")
            em.op("dve", lambda e, pslice=pslice, sn=sn, m2=m2: e.tensor_tensor(
                m2[0:64, :], pslice[64:128, :], sn[0:64, :], ALU.mult),
                reads=[pXr[nh], "sinT"], writes=["csb1"], tag=f"B.r2.{h}.{nh}")
            em.op("dve", lambda e, pslice=pslice, sn=sn, m2=m2: e.tensor_tensor(
                m2[64:128, :], pslice[0:64, :], sn[64:128, :], ALU.mult),
                reads=[pXr[nh], "sinT"], writes=["csb1b"], tag=f"B.r3.{h}.{nh}")
            em.op("dve", lambda e, h=h, nh=nh, qc=qc, m2=m2: e.tensor_tensor(
                qr_head(h)[:, nh * 512:(nh + 1) * 512], qc, m2, ALU.add),
                reads=["csb0", "csb1", "csb1b"] + qres, writes=qres, tag=f"B.r4.{h}.{nh}")

    WIN = []
    for i in range(5):
        qlo = max(0, 256 * i - 256)
        qhi = min(NTOK, 256 * i + 256)
        Areg = (256 * i - 256, 256 * i) if i >= 1 else None
        Breg = (256 * i, 256 * i + 256) if i <= 3 else None
        WIN.append((qlo, qhi - qlo, Areg, Breg))

    def attnT_buf(par, kt):
        base = 4096 + par * 2048
        return SCR[:, base + kt * 512: base + (kt + 1) * 512]

    def attnT_res(par, kt):
        return [f"S{8 + par * 4 + kt}"]

    sc_banks = ["pC", "pD", "pA0", "pA1"]
    sc_ps = [psC[:, :], psD[:, :], psA[:, 0:512], psA[:, 512:1024]]

    sm_banks = [("pE", psE[:, 0:512]), ("pB0", psB[:, 0:512]), ("pA0", psA[:, 0:512])]
    av_banks = [("pF", psF[:, 0:512]), ("pB1", psB[:, 512:1024]), ("pA1", psA[:, 512:1024])]
    sc_bk = [("pC", psC[:, :]), ("pD", psD[:, :])]

    def avsb_buf(k):
        return scr_f32(512 * k, 512)

    def avsb_res(k):
        return [f"S{2 * k}", f"S{2 * k + 1}"]

    def post_pe(p):
        (h, i, par3, par2, qlo, qcnt, Areg, Breg) = p
        smbk, smps = sm_banks[par3]
        avbk, avps = av_banks[par3]
        for kt in range(4):
            ktile = (2 * i + kt) * 128
            em.op("pe", lambda e, par2=par2, kt=kt, qcnt=qcnt, smps=smps, i=i: e.matmul(
                smps[:, 0:qcnt], onesb[:, :], attnT_buf(par2, kt)[:, 0:qcnt],
                start=(kt == 0), stop=(kt == 3 and i != 4)),
                reads=attnT_res(par2, kt) + ["onesb"], writes=[smbk], tag=f"C.sm{h}.{i}.{kt}")
            em.op("pe", lambda e, par2=par2, kt=kt, ktile=ktile, qcnt=qcnt, avps=avps: e.matmul(
                avps[:, 0:qcnt], vt[:, ktile:ktile + 128], attnT_buf(par2, kt)[:, 0:qcnt],
                start=(kt == 0), stop=(kt == 3)),
                reads=attnT_res(par2, kt) + [f"vt{ktile // 128}"], writes=[avbk], tag=f"C.av{h}.{i}.{kt}")
        if i == 4:
            em.op("pe", lambda e, qcnt=qcnt, smps=smps: e.matmul(
                smps[:, 0:qcnt], o1b[:, :], padr[0:1, 512:512 + qcnt],
                start=False, stop=True),
                reads=["padr", "o1b"], writes=[smbk], tag=f"C.pd{h}.{i}")

    def post_tail(p):
        (h, i, par3, par2, qlo, qcnt, Areg, Breg) = p
        smbk, smps = sm_banks[par3]
        avbk, avps = av_banks[par3]
        cb = csb[par3]
        cbr = f"csb{par3}"
        wcl = [cbr, "csb1b"] if par3 == 1 else [cbr]
        wvo = [0, 256, 768, 768, 1280][i]
        em.op("dve", lambda e, cb=cb, qcnt=qcnt, smps=smps: e.reciprocal(cb[:, 0:qcnt], smps[:, 0:qcnt]),
              reads=[smbk], writes=wcl, tag=f"C.rc{h}.{i}")
        cweng = "dve" if (5 * h + i) % 4 == 3 else "pool"
        em.op(cweng, lambda e, cb=cb, qcnt=qcnt, wvo=wvo: e.tensor_tensor(
            cb[:, 0:qcnt], cb[:, 0:qcnt], wvc[:, wvo:wvo + qcnt], ALU.mult),
            reads=[cbr, "wvc"], writes=[cbr], tag=f"C.cw{h}.{i}")
        avs = avsb_buf(par3)
        em.op("act", lambda e, avs=avs, qcnt=qcnt, avps=avps: e.copy(avs[:, 0:qcnt], avps[:, 0:qcnt]),
              reads=[avbk], writes=avsb_res(par3), tag=f"C.ave{h}.{i}")
        tl = tails[h % 4]
        tlr = f"tail{h % 4}"
        if Areg is not None:
            aoff = Areg[0] - qlo
            ares = RA(h * 1024 + Areg[0], h * 1024 + Areg[1])
            em.op("pool", lambda e, cb=cb, aoff=aoff, avs=avs: e.tensor_tensor(
                cb[:, aoff:aoff + 256], cb[:, aoff:aoff + 256], avs[:, aoff:aoff + 256], ALU.mult),
                reads=[cbr] + avsb_res(par3), writes=[cbr], tag=f"C.t1{h}.{i}")
            em.op("pool", lambda e, h=h, Areg=Areg, cb=cb, aoff=aoff, tl=tl: e.tensor_tensor(
                BIGA[:, h * 1024 + Areg[0]: h * 1024 + Areg[1]], cb[:, aoff:aoff + 256], tl[:, :], ALU.add),
                reads=[cbr, tlr] + ares, writes=ares, tag=f"C.bA{h}.{i}")
        if Breg is not None:
            boff = Breg[0] - qlo
            em.op("pool", lambda e, cb=cb, boff=boff, avs=avs, tl=tl: e.tensor_tensor(
                tl[:, :], avs[:, boff:boff + 256], cb[:, boff:boff + 256], ALU.mult),
                reads=[cbr] + avsb_res(par3), writes=[tlr], tag=f"C.bB{h}.{i}")

    w_prefetch(d_wo, 0, "D")
    w_prefetch(d_wo, 1, "D")
    pend = None
    tailq = []
    idx = 0
    for h in range(H):
        for i, (qlo, qcnt, Areg, Breg) in enumerate(WIN):
            par3 = idx % 3
            par2 = idx % 2
            for kt in range(4):
                bk, bp = sc_bk[kt % 2]
                ktile = (2 * i + kt) * 128
                em.op("pe", lambda e, bp=bp, ktile=ktile, h=h, qlo=qlo, qcnt=qcnt: e.matmul(
                    bp[:, 0:qcnt], kr[:, ktile:ktile + 128],
                    qr_head(h)[:, qlo:qlo + qcnt], start=True, stop=True),
                    reads=[f"kr{ktile // 128}"] + RB(h * 1024 + qlo, h * 1024 + qlo + qcnt),
                    writes=[bk], tag=f"C.sc{h}.{i}.{kt}")
                em.op("act", lambda e, bp=bp, par2=par2, kt=kt, qcnt=qcnt: e.activation(
                    attnT_buf(par2, kt)[:, 0:qcnt], bp[:, 0:qcnt], AF.Sigmoid, scale=SCALE),
                    reads=[bk], writes=attnT_res(par2, kt), tag=f"C.sg{h}.{i}.{kt}")
            if pend is not None:
                post_pe(pend)
                tailq.append(pend)
                if len(tailq) == 2:
                    post_tail(tailq.pop(0))
            pend = (h, i, par3, par2, qlo, qcnt, Areg, Breg)
            idx += 1
    post_pe(pend)
    tailq.append(pend)
    for t in tailq:
        post_tail(t)

    def slab_buf(par):
        return scr_f32(par * 1024, 1024)

    def slab_res(par):
        return [f"S{par}"]

    def big_matmul_spill(d_w, d_raw, tagp, lora=False, npre=0):
        spills = []
        for j in range(NCH):
            wres = [f"wb{j % 2}"]
            if j >= npre:
                dma("pool", wbuf[j % 2][:, :], d_w[j, :, :], reads=wres, writes=wres, tag=f"{tagp}.w{j}")
            pX = psA if j % 2 == 0 else psB
            pXr = ["pA0", "pA1"] if j % 2 == 0 else ["pB0", "pB1"]
            for c in range(NCH):
                for nh in range(2):
                    em.op("pe", lambda e, c=c, nh=nh, j=j, pX=pX: e.matmul(
                        pX[:, nh * 512:(nh + 1) * 512], wbuf[j % 2][:, c * 128:(c + 1) * 128],
                        BIGA[:, c * 1024 + nh * 512: c * 1024 + (nh + 1) * 512],
                        start=(c == 0), stop=(c == 31 and not lora)),
                        reads=wres + RA(c * 1024 + nh * 512, c * 1024 + (nh + 1) * 512),
                        writes=[pXr[nh]], tag=f"{tagp}.m{j}.{c}.{nh}")
            if lora:
                for nh in range(2):
                    em.op("pe", lambda e, nh=nh, j=j, pX=pX: e.matmul(
                        pX[:, nh * 512:(nh + 1) * 512], BIGB[:, 4096 + j * 128: 4096 + (j + 1) * 128],
                        SCR[:, 6144 + nh * 512: 6144 + (nh + 1) * 512], start=False, stop=True),
                        reads=RB(4096 + j * 128, 4096 + (j + 1) * 128) + ["babld", "S12", "S13"],
                        writes=[pXr[nh]], tag=f"{tagp}.l{j}.{nh}")
            sl = slab_buf(j % 2)
            slr = slab_res(j % 2)
            if lora:
                em.op("act", lambda e, sl=sl, pX=pX, j=j: e.activation(
                    sl, pX[:, :], AF.Identity, bias=biasl[:, j:j + 1], scale=1.0),
                    reads=pXr + ["biasl"], writes=slr, tag=f"{tagp}.e{j}")
            else:
                em.op("act", lambda e, sl=sl, pX=pX: e.copy(sl, pX[:, :]),
                      reads=pXr, writes=slr, tag=f"{tagp}.e{j}")
            spills.append(dma("sp", d_raw[:, :, j, :], sl.rearrange("p (t n) -> p t n", n=128),
                              reads=slr, writes=[], tag=f"{tagp}.s{j}"))
        return spills

    def xt_bufs(to):
        return (bigb_f32(8192 + 4096 * (to % 2), 4096),
                RB(16384 + 8192 * (to % 2), 24576 + 8192 * (to % 2)))

    def x_from_input(to, xt, xtr):
        dma("actq", xt, d_x[QOFF + to * 128: QOFF + (to + 1) * 128, :],
            reads=xtr, writes=xtr, tag=f"E.x{to}")

    for _t in (0, 1):
        x_from_input(_t, *xt_bufs(_t))

    spillD = big_matmul_spill(d_wo, d_raw1, "D", npre=2)

    def apply_phase(d_raw, tagp, with_ln3, spill_ids, x_src_fn, out_sink_fn, xpre=()):
        def tile_bufs(to):
            rawfm = bigb_f32(4096 * (to % 2), 4096)
            rfr = RB(*xslot_cols(to))
            xt = bigb_f32(8192 + 4096 * (to % 2), 4096)
            xtr = RB(16384 + 8192 * (to % 2), 24576 + 8192 * (to % 2))
            return rawfm, rfr, xt, xtr

        def emit_dmas(to):
            rawfm, rfr, xt, xtr = tile_bufs(to)
            rb = dma("actq", rawfm.rearrange("p (j n) -> p j n", n=128),
                     d_raw[:, to, :, :], reads=rfr, writes=rfr, tag=f"{tagp}.rb{to}")
            em.ops[rb]["deps"].update(spill_ids)
            if to not in xpre:
                x_src_fn(to, xt, xtr)

        all_banks = ["pC", "pD", "pE", "pF", "pA0", "pA1", "pB0", "pB1"]
        deferred = []
        emit_dmas(0)
        for to in range(NT_OWN):
            if to + 1 < NT_OWN:
                emit_dmas(to + 1)
            rawfm, rfr, xt, xtr = tile_bufs(to)
            par = to % 2
            lo0 = 8192 * (to % 2)
            for jg in range(0, 32, 4):
                bank = all_banks[jg // 4]
                bp = PS[bank]
                for j in range(jg, jg + 4):
                    em.op("pe", lambda e, j=j, jg=jg, bp=bp, rawfm=rawfm: e.transpose(
                        bp[:, (j - jg) * 128:(j - jg + 1) * 128], rawfm[:, j * 128:(j + 1) * 128], idf[:, :]),
                        reads=rfr + ["idf"], writes=[bank], tag=f"{tagp}.tr{to}.{j}")
                emit_bn_chunk(bp, [bank], par, jg // 4, f"{tagp}.{to}")
            emit_ln_finish(par, f"{tagp}.{to}")
            em.op("act", lambda e, xt=xt, par=par: e.activation(
                xt, xt, AF.Identity, bias=nbias_[par][:, 0:1], scale=1.0),
                reads=xtr + [f"nbias{par}"], writes=xtr, tag=f"{tagp}.xb{to}")
            for jg in range(0, 32, 4):
                bank = all_banks[jg // 4]
                bp = PS[bank]
                trr = RB(lo0 + jg * 256, lo0 + (jg + 4) * 256)
                em.op("dve", lambda e, bp=bp, jg=jg, rawfm=rawfm, xt=xt, par=par: e.scalar_tensor_tensor(
                    rawfm[:, jg * 128:(jg + 4) * 128], bp[:, :], rstd_[par][:, 0:1],
                    xt[:, jg * 128:(jg + 4) * 128], ALU.mult, ALU.add),
                    reads=[bank, f"rstd{par}"] + xtr + trr, writes=trr, tag=f"{tagp}.st{to}.{jg}")
            if deferred:
                deferred.pop(0)()
            out_sink_fn(to, rawfm, rfr)
            if with_ln3:
                emit_ln_stats(rawfm, rfr, f"{tagp}.3.{to}", par=par)
                lt = SCR[:, 4096 * (to % 2):4096 * (to % 2) + 4096]
                ltr = RS(*ln1t_cols(to))
                em.op("act", lambda e, rawfm=rawfm, lt=lt, par=par: e.activation(
                    lt, rawfm, AF.Identity, bias=nbias_[par][:, 0:1], scale=rstd_[par][:, 0:1]),
                    reads=rfr + [f"rstd{par}", f"nbias{par}"], writes=ltr, tag=f"{tagp}.l3{to}")
                def l3tr(to=to, lt=lt, ltr=ltr):
                    for cg in range(0, 32, 8):
                        bank = all_banks[cg // 8]
                        bview = PS[bank].bitcast(BF16)
                        for c in range(cg, cg + 8):
                            em.op("pe", lambda e, c=c, cg=cg, bv=bview, lt=lt: e.transpose(
                                bv[:, (c - cg) * 128:(c - cg + 1) * 128], lt[:, c * 128:(c + 1) * 128], idb[:, :]),
                                reads=ltr + ["idb"], writes=[bank], tag=f"{tagp}.l3t{to}.{c}")
                        out_ap = BIGA.rearrange("p (c n) -> p c n", n=1024)[:, cg:cg + 8, to * 128:(to + 1) * 128]
                        leng = "dve" if (cg // 8) % 2 == 0 else "act"
                        lfn = (lambda e, bv=bview, oa=out_ap: e.tensor_copy(oa, bv.rearrange("p (c n) -> p c n", n=128))) \
                            if leng == "dve" else \
                            (lambda e, bv=bview, oa=out_ap: e.copy(oa, bv.rearrange("p (c n) -> p c n", n=128)))
                        em.op(leng, lfn,
                            reads=[bank], writes=RA(cg * 1024, (cg + 8) * 1024), tag=f"{tagp}.l3e{to}.{cg}")
                deferred.append(l3tr)
        while deferred:
            deferred.pop(0)()

    def flush_deferred():
        pass

    x2spills = {}

    def x2_sink(to, x2ap, x2res):
        x2spills[to] = dma("sp", d_x2[to], x2ap, reads=x2res, writes=[], tag=f"E.x2s{to}")

    apply_phase(d_raw1, "E", True, spillD, x_from_input, x2_sink, xpre=(0, 1))

    aab = BIGB[:, 0:4096]
    bab = BIGB[:, 4096:8192]
    hw_ap = SCR[:, 6144:7168]
    c128 = scr_f32(2048, 1024)

    dma("pool", aab, d_aa[:, :], reads=RB(0, 4096), writes=RB(0, 4096) + ["aabld"], tag="F.aa")
    dma("pool", bab, d_ba[:, :], reads=RB(4096, 8192), writes=RB(4096, 8192) + ["babld"], tag="F.ba")
    w_prefetch(d_wb, 0, "G")
    w_prefetch(d_wb, 1, "G")

    for c in range(NCH):
        for nh, (bk, bp) in enumerate([("pE", psE), ("pF", psF)]):
            em.op("pe", lambda e, c=c, nh=nh, bp=bp: e.matmul(
                bp[0:8, 0:512], wrb[:, c * 8:(c + 1) * 8],
                BIGA[:, c * 1024 + nh * 512: c * 1024 + (nh + 1) * 512],
                start=(c == 0), stop=(c == 31)),
                reads=["wrb"] + RA(c * 1024 + nh * 512, c * 1024 + (nh + 1) * 512),
                writes=[bk], tag=f"F.r{c}.{nh}")
    em.op("dve", lambda e: e.tensor_copy(lgT[:, 0:512], psE[0:8, 0:512]),
          reads=["pE"], writes=["lgT.0", "S0", "S1"], tag="F.le0")
    em.op("dve", lambda e: e.tensor_copy(lgT[:, 512:1024], psF[0:8, 0:512]),
          reads=["pF"], writes=["lgT.1", "S2", "S3"], tag="F.le1")
    for g in range(2):
        bank = tr_banks[g]
        bp = PS[bank]
        for c8 in range(g * 4, g * 4 + 4):
            em.op("pe", lambda e, c8=c8, g=g, bp=bp: e.transpose(
                bp[:, (c8 - g * 4) * 8:(c8 - g * 4 + 1) * 8], lgT[:, c8 * 128:(c8 + 1) * 128], idf[0:8, 0:8]),
                reads=[f"lgT.{c8 // 4}", "S0", "S1", "S2", "S3", "idf"], writes=[bank], tag=f"F.lt{c8}")
        em.op("dve", lambda e, bp=bp, g=g: e.tensor_copy(lgtm[:, g * 32:(g + 1) * 32], bp[:, 0:32]),
              reads=[bank], writes=[f"lgtm.{g}"], tag=f"F.lte{g}")
    for c8 in range(8):
        lg8 = lgtm[:, c8 * 8:(c8 + 1) * 8]
        em.op("dve", lambda e, lg8=lg8: e.reduce_max(mx1[:, :], lg8, AX.X),
              reads=[f"lgtm.{c8 // 4}"], writes=["mx1"], tag=f"F.m{c8}")
        em.op("dve", lambda e: e.tensor_scalar(mx1[:, :], mx1[:, :], -1.0, None, ALU.mult),
              reads=["mx1"], writes=["mx1"], tag=f"F.n{c8}")
        em.op("act", lambda e, lg8=lg8: e.activation(ee8[:, :], lg8, AF.Exp, bias=mx1[:, 0:1], scale=1.0),
              reads=[f"lgtm.{c8 // 4}", "mx1"], writes=["ee8"], tag=f"F.e{c8}")
        em.op("dve", lambda e: e.max(out=mx8[:, :], in_=ee8[:, :]),
              reads=["ee8"], writes=["mx8"], tag=f"F.x8{c8}")
        em.op("dve", lambda e: e.memset(scr8[:, :], 0.0), reads=[], writes=["scr8"], tag=f"F.z{c8}")
        em.op("dve", lambda e: e.tensor_copy(scr8[:, 0:2], mx8[:, 0:2]),
              reads=["mx8"], writes=["scr8"], tag=f"F.c2{c8}")
        em.op("dve", lambda e: e.match_replace(out=mr8[:, :], in_to_replace=scr8[:, :],
                                               in_values=ee8[:, :], imm_value=0.0),
              reads=["scr8", "ee8"], writes=["mr8"], tag=f"F.mr{c8}")
        em.op("dve", lambda e: e.tensor_sub(mr8[:, :], ee8[:, :], mr8[:, :]),
              reads=["ee8", "mr8"], writes=["mr8"], tag=f"F.k{c8}")
        em.op("dve", lambda e: e.tensor_tensor(den[:, :], mx8[:, 0:1], mx8[:, 1:2], ALU.add),
              reads=["mx8"], writes=["den"], tag=f"F.d{c8}")
        em.op("dve", lambda e: e.reciprocal(rden1[:, :], den[:, :]),
              reads=["den"], writes=["rden1"], tag=f"F.rd{c8}")
        em.op("dve", lambda e, c8=c8: e.tensor_scalar(cmbtm[:, c8 * 8:(c8 + 1) * 8], mr8[:, :],
                                                      rden1[:, 0:1], 2.0, ALU.mult, ALU.mult),
              reads=["mr8", "rden1"], writes=[f"cmbtm.{c8}"], tag=f"F.cm{c8}")
    for g in range(2):
        bank = tr_banks[2 + g]
        bp = PS[bank]
        for c8 in range(g * 4, g * 4 + 4):
            em.op("pe", lambda e, c8=c8, bp=bp: e.transpose(
                bp[0:8, (c8 % 4) * 128:(c8 % 4 + 1) * 128], cmbtm[:, c8 * 8:(c8 + 1) * 8], idf[:, :]),
                reads=[f"cmbtm.{c8}", "idf"], writes=[bank], tag=f"F.ct{c8}")
        em.op("dve", lambda e, bp=bp, g=g: e.tensor_copy(cmbT[:, g * 512:(g + 1) * 512], bp[0:8, 0:512]),
              reads=[bank, f"lgT.{g}"], writes=[f"cmbT.{g}", f"lgT.{g}", f"S{2*g}", f"S{2*g+1}"], tag=f"F.cte{g}")
    for nh, (bk, bp) in enumerate([("pE", psE), ("pF", psF)]):
        em.op("pe", lambda e, nh=nh, bp=bp: e.matmul(
            bp[:, 0:512], e8f[:, :], cmbT[0:8, nh * 512:(nh + 1) * 512], start=True, stop=True),
            reads=[f"cmbT.{nh}", f"S{2*nh}", f"S{2*nh+1}", "e8f"], writes=[bk], tag=f"F.ex{nh}")
        em.op("dve", lambda e, nh=nh, bp=bp, c128=c128: e.tensor_copy(
            c128[:, nh * 512:(nh + 1) * 512], bp[:, 0:512]),
            reads=[bk], writes=[f"c128.{nh}", f"S{8 + 2 * nh}", f"S{9 + 2 * nh}"], tag=f"F.exe{nh}")
    for c in range(NCH):
        for nh in range(2):
            em.op("pe", lambda e, c=c, nh=nh: e.matmul(
                psB[:, nh * 512:(nh + 1) * 512], aab[:, c * 128:(c + 1) * 128],
                BIGA[:, c * 1024 + nh * 512: c * 1024 + (nh + 1) * 512],
                start=(c == 0), stop=(c == 31)),
                reads=["aabld"] + RB(c * 128, (c + 1) * 128) + RA(c * 1024 + nh * 512, c * 1024 + (nh + 1) * 512),
                writes=[f"pB{nh}"], tag=f"F.h{c}.{nh}")
    for nh in range(2):
        em.op("dve", lambda e, nh=nh, c128=c128: e.tensor_tensor(
            hw_ap[:, nh * 512:(nh + 1) * 512], psB[:, nh * 512:(nh + 1) * 512],
            c128[:, nh * 512:(nh + 1) * 512], ALU.mult),
            reads=[f"pB{nh}", f"c128.{nh}"], writes=["S12", "S13"], tag=f"F.hw{nh}")

    st_ids = []

    def x2_from_scratch(to, xt, xtr):
        rb = dma("sp", xt, d_x2[to], reads=xtr, writes=xtr, tag=f"H.x2{to}")
        em.ops[rb]["deps"].add(x2spills[to])

    for _t in (0, 1):
        x2_from_scratch(_t, *xt_bufs(_t))

    spillG = big_matmul_spill(d_wb, d_raw2, "G", lora=True, npre=2)


    def out_sink(to, outap, outres):
        st_ids.append(dma("st", d_out[to * 128:(to + 1) * 128, :], outap,
                          reads=outres, writes=[], tag=f"H.out{to}"))

    apply_phase(d_raw2, "H", False, spillG, x2_from_scratch, out_sink, xpre=(0, 1))

    fin = em.op("sp", lambda e: e.nop(), reads=[], writes=[], tag="fin")
    em.ops[fin]["deps"].update(st_ids)

    streams = em.build(sems, dsems)

    with stack, nc.Block() as block:
        @block.tensor
        def _(e):
            for f in streams["pe"]:
                f(e)

        @block.scalar
        def _(e):
            for f in streams["act"]:
                f(e)

        @block.vector
        def _(e):
            for f in streams["dve"]:
                f(e)

        @block.gpsimd
        def _(e):
            for f in streams["pool"]:
                f(e)

        @block.sync
        def _(e):
            for f in streams["sp"]:
                f(e)

    return nc



def _tile_weight(w):
    t = w.reshape(32, 128, 32, 128)
    t = np.ascontiguousarray(t.transpose(0, 3, 2, 1))
    return t.reshape(32, 128, 32 * 128)


def _tile_small(w):
    O = w.shape[0]
    t = w.reshape(O, 32, 128)
    t = np.ascontiguousarray(t.transpose(2, 1, 0))
    return t.reshape(128, 32 * O)


def _rope_tables(lo, n):
    inv = 1.0 / (10000.0 ** (np.arange(0, 128, 2, dtype=np.float32) / 128.0))
    t = np.arange(lo, lo + n, dtype=np.float32)
    f = np.outer(t, inv)
    cos = np.zeros((n, 128), dtype=np.float32)
    sin = np.zeros((n, 128), dtype=np.float32)
    cos[:, 0::2] = np.cos(f)
    cos[:, 1::2] = np.cos(f)
    sin[:, 0::2] = np.sin(f)
    sin[:, 1::2] = np.sin(f)
    sin_eff = sin.copy()
    sin_eff[:, 0:64] = -sin_eff[:, 0:64]
    return np.ascontiguousarray(cos.T), np.ascontiguousarray(sin_eff.T)


def kernel(L0_attn_x, L0_attn_Wq, L0_attn_Wk, L0_attn_Wv, L0_attn_Wo,
           L0_ffn_W_base, L0_ffn_b_base, L0_ffn_W_router, L0_ffn_A, L0_ffn_B):
    x = np.asarray(L0_attn_x, dtype=np.float32)
    wq_l = _tile_weight(np.asarray(L0_attn_Wq, dtype=np.float32))
    wo_l = _tile_weight(np.asarray(L0_attn_Wo, dtype=np.float32))
    wb_l = _tile_weight(np.asarray(L0_ffn_W_base, dtype=np.float32))
    wk_l = _tile_small(np.asarray(L0_attn_Wk, dtype=np.float32))
    wv_l = _tile_small(np.asarray(L0_attn_Wv, dtype=np.float32))
    wr_l = _tile_small(np.asarray(L0_ffn_W_router, dtype=np.float32))
    A = np.asarray(L0_ffn_A, dtype=np.float32)
    Bm = np.asarray(L0_ffn_B, dtype=np.float32)
    aa_l = np.ascontiguousarray(A.reshape(128, 32, 128).transpose(2, 1, 0)).reshape(128, 4096)
    ba_l = np.ascontiguousarray(Bm.transpose(0, 2, 1)).reshape(128, 4096)
    bb_l = np.ascontiguousarray(np.asarray(L0_ffn_b_base, dtype=np.float32).reshape(32, 128).T)

    alpha = np.linspace(0.0, 1.0, 256, dtype=np.float32)
    idb = np.eye(128, dtype=NPBF16)
    idf = np.eye(128, dtype=np.float32)
    onesc = np.ones((128, 128), dtype=NPBF16)
    o1c = np.ones((1, 128), dtype=NPBF16)
    e8 = np.zeros((8, 128), dtype=np.float32)
    for ei in range(8):
        e8[ei, ei * 16:(ei + 1) * 16] = 1.0

    in_maps = []
    for c in range(8):
        b, half = c // 2, c % 2
        qstart = half * 1024
        lo = qstart - 256
        xs = np.zeros((NSH, D), dtype=np.float32)
        glo, ghi = max(0, lo), min(S, lo + NSH)
        xs[glo - lo: ghi - lo] = x[b, glo:ghi]
        cosT, sinT = _rope_tables(lo, NSH)
        wv = np.zeros((128, 1536), dtype=np.float32)
        wv[:, 0:256] = 0.0 if half == 0 else (1.0 - alpha)
        wv[:, 256:512] = 1.0 if half == 0 else alpha
        wv[:, 512:768] = 1.0 - alpha
        wv[:, 768:1024] = alpha
        wv[:, 1024:1280] = 1.0 - alpha
        wv[:, 1280:1536] = alpha
        padc = np.zeros((1, 1024), dtype=NPBF16)
        if half == 1:
            padc[0, 512:768] = NPBF16(-128.0)
        in_maps.append({
            "x_sh": xs, "wq_l": wq_l, "wo_l": wo_l, "wb_l": wb_l,
            "wk_l": wk_l, "wv_l": wv_l, "wr_l": wr_l,
            "aa_l": aa_l, "ba_l": ba_l, "bb_l": bb_l,
            "cos_l": cosT, "sin_l": sinT, "wv_c": wv, "pad_c": padc,
            "idb_c": idb, "idf_c": idf, "ones_c": onesc, "o1_c": o1c, "e8_c": e8,
        })

    nc = build_program()
    res = run_bass_kernel_spmd(nc, in_maps, core_ids=list(range(8)))
    out = np.empty((B_, S, D), dtype=np.float32)
    for c in range(8):
        b, half = c // 2, c % 2
        out[b, half * 1024:(half + 1) * 1024] = res.results[c]["out_sh"]
    return out
